# version 13 (final)
# baseline (speedup 1.0000x reference)
import numpy as np
import concourse.bass as bass
import concourse.mybir as mybir
from concourse.alu_op_type import AluOpType as OP
from concourse.bass_utils import run_bass_kernel_spmd
from contextlib import ExitStack

F32 = mybir.dt.float32
BF16 = mybir.dt.bfloat16
AF = mybir.ActivationFunctionType
AXX = mybir.AxisListType.X

NCH = 4
NEGF = -1e9


class Sem:
    def __init__(self, h, name):
        self.h, self.name, self.n = h, name, 0


class Rot:
    def __init__(self, n):
        self.n, self.rel, self.i = n, [None] * n, 0

    def acquire(self):
        s = self.i % self.n
        self.i += 1
        w = self.rel[s]
        return s, ([w] if w is not None else [])

    def release(self, slot, sem, val):
        self.rel[slot] = (sem, val)


class Stream:

    def __init__(self, sems):
        self.sems = sems
        self.i = 0

    def dma(self, P, eng, dst, src, waits=()):
        s = self.sems[self.i % len(self.sems)]
        self.i += 1
        P.op(eng, lambda e, d=dst, s_=src: e.dma_start(out=d, in_=s_),
             waits=waits, inc=s, incn=16)
        return (s, s.n)


class Plan:
    ENGS = ("te", "ve", "se", "sy", "ge")

    def __init__(self):
        self.ops = {k: [] for k in self.ENGS}
        self.last_wait = {}

    def op(self, eng, fn, waits=(), inc=None, incn=1):
        w = []
        for (s, v) in waits:
            if v > self.last_wait.get((eng, s.name), -1):
                self.last_wait[(eng, s.name)] = v
                w.append((s, v))
        if inc is not None:
            inc.n += incn

        def run(e, fn=fn, w=tuple(w), inc=inc, incn=incn):
            for (s, v) in w:
                e.wait_ge(s.h, v)
            ins = fn(e)
            if inc is not None:
                ins.then_inc(inc.h, incn)
        self.ops[eng].append(run)

    def drain(self, eng):
        self.ops[eng].append(lambda e: e.drain())

    def wait(self, eng, *waits):
        self.op(eng, lambda e: None, waits=waits)

    def emit(self, nc, name):
        with nc.Block(name, no_gpsimd_drain=True) as blk:
            @blk.tensor
            def _(e):
                for f in self.ops["te"]:
                    f(e)

            @blk.vector
            def _(e):
                for f in self.ops["ve"]:
                    f(e)

            @blk.scalar
            def _(e):
                for f in self.ops["se"]:
                    f(e)

            @blk.sync
            def _(e):
                for f in self.ops["sy"]:
                    f(e)

            @blk.gpsimd
            def _(e):
                for f in self.ops["ge"]:
                    f(e)


def R3(a):
    return a.rearrange("(n p) m -> p n m", p=128)


class Builder:
    def __init__(self, p0_nch=NCH, p0_pool=True, p0_attn=True, p0_proj=True,
                 p0_st=6, p2_mode=4, p15_mode=3):
        self.p0_nch = p0_nch
        self.p0_pool = p0_pool
        self.p0_attn = p0_attn
        self.p0_proj = p0_proj
        self.p0_st = p0_st
        self.p2_mode = p2_mode
        self.p15_mode = p15_mode
        self.nc = bass.Bass("TRN2", target_bir_lowering=False)
        self.ctx = ExitStack()

    def declare(self):
        nc = self.nc
        DT = nc.dram_tensor
        g = self.d = {}
        for nm, shp in [("x_all", [16384, 1024]), ("x_own", [2048, 1024]),
                        ("x_halo", [128, 1024]), ("qkv_w", [1024, 3072]),
                        ("rq_w", [1024, 256]), ("rkv_w", [1024, 512]),
                        ("rout_w", [256, 1024]), ("out_w", [1024, 1024]),
                        ("router_wT", [1024, 16]), ("gate_w", [16, 1024, 256]),
                        ("up_w", [16, 1024, 256]), ("down_w", [16, 256, 1024]),
                        ("sgw_t", [1024, 2048]), ("suw_t", [1024, 2048]),
                        ("sdw_t", [2048, 1024]), ("ident_in", [128, 128]),
                        ("zpool_in", [128, 129]), ("rep2_in", [2, 128]),
                        ("ue16_in", [16, 2048]), ("glgr_in", [128, 4]),
                        ("maskAB_in", [128, 256]), ("maskA0B_in", [128, 256]),
                        ("latmask_in", [128, 16, 64]), ("qkvbt_in", [128, 24]),
                        ("rqbt_in", [128, 2]), ("rkvbk_in", [128, 2]),
                        ("rvb_in", [64, 256]), ("routbt_in", [128, 8]),
                        ("outbrep_in", [128, 1024]), ("vbrep_in", [128, 1024]),
                        ("corr_in", [128, 16])]:
            g[nm] = DT(nm, shp, F32, kind="ExternalInput").ap()
        dbg_kind = "ExternalOutput" if getattr(self, "dbg_spills", False) else "Internal"
        for nm in ("local_sp", "y0_sp"):
            pass
        for nm, shp, dt in [("local_sp", [128, 8, 2048], BF16),
                            ("y0_sp", [16, 128, 1024], F32),
                            ("y1_sp", [16, 128, 1024], F32),
                            ("y1fm_sp", [128, 8, 2048], BF16),
                            ("routedy_sp", [16, 128, 1024], F32),
                            ("y2pre_sp", [16, 128, 1024], F32),
                            ("dwbf_sp", [16, 128, 2, 1024], BF16),
                            ("uwbf_sp", [16, 128, 8, 256], BF16),
                            ("swg_sp", [128, 8, 2048], BF16),
                            ("swu_sp", [128, 8, 2048], BF16),
                            ("swd_sp", [128, 16, 1024], BF16)]:
            g[nm] = DT(nm, shp, dt,
                       kind=(dbg_kind if nm in ("local_sp", "y0_sp")
                             else "Internal")).ap()
        g["out"] = DT("out", [2048, 1024], F32, kind="ExternalOutput").ap()

        S = lambda n, s, t: self.ctx.enter_context(nc.sbuf_tensor(n, s, t))
        self.ident = S("ident", [128, 128], BF16)
        self.zpool = S("zpool", [128, 129], BF16)
        self.rep2 = S("rep2", [2, 128], BF16)
        self.ue16 = S("ue16", [16, 2048], BF16)
        self.glgr = S("glgr", [128, 4], BF16)
        self.maskAB = S("maskAB", [128, 256], F32)
        self.maskA0B = S("maskA0B", [128, 256], F32)
        self.latmask = S("latmask", [128, 16, 64], F32)
        self.qkvbt = S("qkvbt", [128, 24], F32)
        self.rqbt = S("rqbt", [128, 2], F32)
        self.rkvbk = S("rkvbk", [128, 2], F32)
        self.rvb = S("rvb", [64, 256], F32)
        self.routbt = S("routbt", [128, 8], F32)
        self.outbrep = S("outbrep", [128, 1024], F32)
        self.vbrep = S("vbrep", [128, 1024], F32)
        self.corr = S("corr", [128, 16], F32)
        self.qb8 = S("qb8", [128, 8], F32)
        self.epsv = S("epsv", [128, 1], F32)
        self.rqb8 = S("rqb8", [128, 2], F32)
        self.rq_sb = S("rq_sb", [128, 2, 2048], BF16)
        self.pooledfm = S("pooledfm", [128, 8, 64], BF16)
        self.rkfm = S("rkfm", [128, 2, 64], BF16)
        self.rvlm = S("rvlm", [64, 256], BF16)
        self.combT = S("combT", [16, 2048], BF16)

        self.sem = {}
        for nm in ("pe", "pp", "ve", "se"):
            self.sem[nm] = Sem(self.ctx.enter_context(nc.semaphore(nm)), nm)

    def new_sem(self, nm):
        return Sem(self.ctx.enter_context(self.nc.semaphore(nm)), nm)

    def new_stream(self, nm, n):
        return Stream([self.new_sem(f"{nm}{i}") for i in range(n)])

    def phase0(self):
        nc, d, sm = self.nc, self.d, self.sem
        pe, pp, ve, se = (sm[k] for k in ("pe", "pp", "ve", "se"))
        c0 = ExitStack()
        S = lambda n, s, t: c0.enter_context(nc.sbuf_tensor(n, s, t))
        PS = lambda n, s, t: c0.enter_context(nc.psum_tensor(n, s, t))
        self.ca = ExitStack()
        SA = lambda n, s, t: self.ca.enter_context(nc.sbuf_tensor(n, s, t))
        rkvw = SA("rkvw", [128, 8, 512], BF16)
        self.routw = SA("routw", [128, 2, 1024], BF16)
        self.outw = SA("outw", [128, 8, 1024], BF16)
        self.routerw = SA("routerw", [128, 8, 16], BF16)

        qkvw = S("qkvw", [128, 8, 3072], BF16)
        rqw = S("rqw", [128, 8, 256], BF16)
        xown = S("xown", [128, 4, 1024], BF16)
        xhalo = S("xhalo", [128, 1024], BF16)
        xfm = S("xfm", [128, 8, 512], BF16)
        xfmh = S("xfmh", [128, 8, 128], BF16)
        q_sb = S("q_sb", [128, 8, 512], BF16)
        k_sb = S("k_sb", [128, 8, 512], BF16)
        ktail = S("ktail", [128, 8, 128], BF16)
        v_sb = S("v_sb", [128, 4, 1024], BF16)
        vtail = S("vtail", [128, 1024], BF16)
        local_sb = S("local_sb", [128, 8, 512], BF16)
        RN = 10
        ring = S("ring", [128, RN, 1024], BF16)
        maskedb = S("maskedb", [128, 2, 512], F32)
        p_b = S("p_b", [128, 2, 512], BF16)
        pn_b = S("pn_b", [128, 2, 512], BF16)
        pT_b = S("pT_b", [128, 2, 512], BF16)
        macc = S("macc", [128, 16], F32)
        mn_sb = S("mn_sb", [128, 16], F32)
        ssum = S("ssum", [128, 16], F32)
        rvv = S("rvv", [128, 16], F32)
        pooledlm = S("pooledlm", [64, 1024], BF16)

        ps_pool = PS("ps_pool", [64, 1024], F32)
        ps_pr = [PS(f"ps_pr{i}", [128, 512], F32) for i in range(2)]
        ps_sc = [PS(f"ps_sc{i}", [128, 512], F32) for i in range(2)]
        ps_bt = [PS(f"ps_bt{i}", [128, 512], BF16) for i in range(2)]

        P = Plan()
        st_gec = self.new_stream("gec", 1)
        st_gec.dma(P, "ge", self.ident[:], d["ident_in"])
        st_gec.dma(P, "ge", self.zpool[:], d["zpool_in"])
        st_gec.dma(P, "ge", self.glgr[:], d["glgr_in"])
        st_gec.dma(P, "ge", self.rep2[:], d["rep2_in"])
        tk_gec = st_gec.dma(P, "ge", self.ue16[:], d["ue16_in"])
        self.tk_gec = tk_gec
        s_eps = self.new_sem("eps")
        P.op("ge", lambda e: e.memset(self.epsv[:], 1e-5), inc=s_eps)
        self.tk_eps = (s_eps, 1)
        s_qkvw = self.new_sem("sqkvw")
        st1 = Stream([s_qkvw])
        tk_qkvw = st1.dma(P, "ge", qkvw[:], R3(d["qkv_w"]))
        tk_rqw = Stream([self.new_sem("srqw")]).dma(P, "ge", rqw[:], R3(d["rq_w"]))
        s_halo = Stream([self.new_sem("shalo")])
        tk_halo = s_halo.dma(P, "ge", xhalo[:], d["x_halo"])
        st_xo = self.new_stream("xo", 2)
        ld_own = {0: st_xo.dma(P, "ge", xown[:], R3(d["x_own"][0:512, :]))}
        self.tk_rkvw = Stream([self.new_sem("srkvw")]).dma(P, "ge", rkvw[:],
                                                           R3(d["rkv_w"]))
        self.tk_routw = Stream([self.new_sem("sroutw")]).dma(
            P, "ge", self.routw[:], R3(d["rout_w"]))
        self.tk_outw = Stream([self.new_sem("soutw")]).dma(
            P, "ge", self.outw[:], R3(d["out_w"]))
        self.tk_routerw = Stream([self.new_sem("srouterw")]).dma(
            P, "ge", self.routerw[:], R3(d["router_wT"]))
        st_syc = self.new_stream("syc", 1)
        for dst, src in [(self.maskAB[:], d["maskAB_in"]),
                         (self.maskA0B[:], d["maskA0B_in"]),
                         (self.latmask[:], d["latmask_in"]),
                         (self.qkvbt[:], d["qkvbt_in"]),
                         (self.rqbt[:], d["rqbt_in"]),
                         (self.rkvbk[:], d["rkvbk_in"]),
                         (self.rvb[:], d["rvb_in"]),
                         (self.routbt[:], d["routbt_in"]),
                         (self.outbrep[:], d["outbrep_in"]),
                         (self.vbrep[:], d["vbrep_in"]),
                         (self.corr[:], d["corr_in"])]:
            self.tk_consts = st_syc.dma(P, "sy", dst, src)

        P.op("se", lambda e: e.activation(out=self.qb8[:], in_=self.qkvbt[:, 0:8],
                                          func=AF.Identity, scale=0.125),
             waits=[self.tk_consts], inc=se)
        P.op("se", lambda e: e.activation(out=self.rqb8[:], in_=self.rqbt[:],
                                          func=AF.Identity, scale=0.125), inc=se)
        se_bias = se.n

        ring_ldx = {}
        ring_emitted = [0]
        pool_emitted = [0]
        st_xr = self.new_stream("xr", 5)

        def emit_ring(upto_pair):
            while ring_emitted[0] < min(upto_pair, 64):
                j = ring_emitted[0]
                w = []
                need = 2 * j + 2 - RN
                if need > 0:
                    w.append((pp, need))
                ring_ldx[j] = st_xr.dma(
                    P, "ge", ring[:, (2 * j) % RN:(2 * j) % RN + 2, :],
                    R3(d["x_all"][2 * j * 128:(2 * j + 2) * 128, :]), waits=w)
                ring_emitted[0] += 1

        def emit_pool(n):
            if not self.p0_pool:
                return
            for _ in range(n):
                t = pool_emitted[0]
                if t >= 128:
                    return
                emit_ring(t // 2 + RN // 2)
                w = [ring_ldx[t // 2], tk_gec]
                lh = self.zpool[:, 64 - t // 2:128 - t // 2]
                for hf in range(2):
                    P.op("te", lambda e, lh=lh, t=t, hf=hf: e.matmul(
                        ps_pool[0:64, hf * 512:(hf + 1) * 512], lh,
                        ring[:, t % RN, hf * 512:(hf + 1) * 512],
                        start=(t == 0), stop=(t == 127)),
                        waits=w if hf == 0 else (),
                        inc=pp if hf == 1 else None)
                    w = ()
                pool_emitted[0] += 1

        rot_pr, rot_bt = Rot(2), Rot(2)

        def transpose_group(srcs, dsts, wait_extra=()):
            slot, w = rot_bt.acquire()
            w = list(w) + list(wait_extra) + [tk_gec]
            for i, a in enumerate(srcs):
                P.op("te", lambda e, sl=slot, i=i, a=a: e.transpose(
                    ps_bt[sl][:, i * 128:(i + 1) * 128], a, self.ident[:]),
                    waits=w if i == 0 else (),
                    inc=pe if i == len(srcs) - 1 else None)
                w = ()
            mmv = pe.n
            for i, dd in enumerate(dsts):
                P.op("se", lambda e, sl=slot, i=i, dd=dd: e.copy(
                    out=dd, in_=ps_bt[sl][:, i * 128:(i + 1) * 128]),
                    waits=[(pe, mmv)], inc=se if i == len(dsts) - 1 else None)
            rot_bt.release(slot, se, se.n)

        def proj(lhs_fn, rhs_fn, nk, evac, wait_first=(), ncols=512, nrows=128):
            slot, w = rot_pr.acquire()
            w = list(w) + list(wait_first)
            for kk in range(nk):
                P.op("te", lambda e, sl=slot, kk=kk, nc2=ncols, nr=nrows: e.matmul(
                    ps_pr[sl][0:nr, 0:nc2], lhs_fn(kk), rhs_fn(kk),
                    start=(kk == 0), stop=(kk == nk - 1)),
                    waits=w if kk == 0 else (),
                    inc=pe if kk == nk - 1 else None)
                w = ()
            evac(slot, pe.n)

        rel_masked = [None, None]
        rel_p = [None, None]
        rel_pn = [None, None]
        rel_pT = [None, None]
        locsb_rel = [None]
        rel_sc = [None, None]
        st_locsp = self.new_stream("locsp", 1)

        for c in range(self.p0_nch):
            halo_se = None
            if c == 0:
                for fc in range(8):
                    transpose_group([xhalo[:, fc * 128:(fc + 1) * 128]],
                                    [xfmh[:, fc, :]],
                                    wait_extra=[tk_halo])
                halo_se = se.n
            for fc in range(8):
                transpose_group(
                    [xown[:, tt, fc * 128:(fc + 1) * 128] for tt in range(4)],
                    [xfm[:, fc, tt * 128:(tt + 1) * 128] for tt in range(4)],
                    wait_extra=[ld_own[c]])
            tr_se = se.n
            tr_pe = pe.n
            if c + 1 < NCH:
                ld_own[c + 1] = st_xo.dma(
                    P, "ge", xown[:], R3(d["x_own"][(c + 1) * 512:(c + 2) * 512, :]),
                    waits=[(pe, tr_pe)])

            for of in (range(8) if self.p0_proj else []):
                def ev(slot, mmv, of=of):
                    P.op("se", lambda e, sl=slot, of=of: e.activation(
                        out=q_sb[:, of, :], in_=ps_pr[sl][:], func=AF.Identity,
                        scale=0.125, bias=self.qb8[:, of:of + 1]),
                        waits=[(pe, mmv), (se, se_bias)], inc=se)
                    rot_pr.release(slot, se, se.n)
                proj(lambda kk, of=of: qkvw[:, kk, of * 128:(of + 1) * 128],
                     lambda kk: xfm[:, kk, :], 8, ev,
                     wait_first=[(se, tr_se), tk_qkvw])

            if c == 0 and self.p0_proj:
                for of in range(8):
                    def ev(slot, mmv, of=of):
                        P.op("se", lambda e, sl=slot, of=of: e.activation(
                            out=ktail[:, of, :], in_=ps_pr[sl][:, 0:128],
                            func=AF.Identity, bias=self.qkvbt[:, 8 + of:9 + of]),
                            waits=[(pe, mmv)], inc=se)
                        rot_pr.release(slot, se, se.n)
                    proj(lambda kk, of=of: qkvw[:, kk, 1024 + of * 128:1152 + of * 128],
                         lambda kk: xfmh[:, kk, :], 8, ev,
                         wait_first=[(se, halo_se)], ncols=128)
                for nh in range(2):
                    def ev(slot, mmv, nh=nh):
                        P.op("ve", lambda e, sl=slot, nh=nh: e.tensor_tensor(
                            out=vtail[:, nh * 512:(nh + 1) * 512],
                            in0=ps_pr[sl][:],
                            in1=self.vbrep[:, nh * 512:(nh + 1) * 512], op=OP.add),
                            waits=[(pe, mmv), self.tk_consts], inc=ve)
                        rot_pr.release(slot, ve, ve.n)
                    proj(lambda kk: xfmh[:, kk, :],
                         lambda kk, nh=nh: qkvw[:, kk, 2048 + nh * 512:2560 + nh * 512],
                         8, ev)
            elif self.p0_proj:
                P.op("ve", lambda e: e.tensor_copy(out=ktail[:],
                                                   in_=k_sb[:, :, 384:512]),
                     waits=[(pe, pe.n)], inc=ve)
                P.op("ve", lambda e: e.tensor_copy(out=vtail[:], in_=v_sb[:, 3, :]),
                     inc=ve)
            tail_ve = ve.n

            for of in (range(8) if self.p0_proj else []):
                def ev(slot, mmv, of=of):
                    P.op("se", lambda e, sl=slot, of=of: e.activation(
                        out=k_sb[:, of, :], in_=ps_pr[sl][:], func=AF.Identity,
                        bias=self.qkvbt[:, 8 + of:9 + of]),
                        waits=[(pe, mmv), (ve, tail_ve)], inc=se)
                    rot_pr.release(slot, se, se.n)
                proj(lambda kk, of=of: qkvw[:, kk, 1024 + of * 128:1152 + of * 128],
                     lambda kk: xfm[:, kk, :], 8, ev)
            k_se = se.n

            for mt in (range(4) if self.p0_proj else []):
                for nh in range(2):
                    def ev(slot, mmv, mt=mt, nh=nh):
                        P.op("ve", lambda e, sl=slot, mt=mt, nh=nh: e.tensor_tensor(
                            out=v_sb[:, mt, nh * 512:(nh + 1) * 512],
                            in0=ps_pr[sl][:],
                            in1=self.vbrep[:, nh * 512:(nh + 1) * 512], op=OP.add),
                            waits=[(pe, mmv), (ve, tail_ve), self.tk_consts],
                            inc=ve)
                        rot_pr.release(slot, ve, ve.n)
                    proj(lambda kk, mt=mt: xfm[:, kk, mt * 128:(mt + 1) * 128],
                         lambda kk, nh=nh: qkvw[:, kk, 2048 + nh * 512:2560 + nh * 512],
                         8, ev)
            v_ve = ve.n

            for of2 in (range(2) if self.p0_proj else []):
                def ev(slot, mmv, of2=of2, c=c):
                    P.op("se", lambda e, sl=slot, of2=of2, c=c: e.activation(
                        out=self.rq_sb[:, of2, c * 512:(c + 1) * 512],
                        in_=ps_pr[sl][:], func=AF.Identity, scale=0.125,
                        bias=self.rqb8[:, of2:of2 + 1]),
                        waits=[(pe, mmv)], inc=se)
                    rot_pr.release(slot, se, se.n)
                proj(lambda kk, of2=of2: rqw[:, kk, of2 * 128:(of2 + 1) * 128],
                     lambda kk: xfm[:, kk, :], 8, ev,
                     wait_first=[tk_rqw])

            emit_pool(8)

            G = 32
            smm = [0] * G
            ttrv = [0] * G
            expv = [0] * G
            nrm = [0] * G
            evT = [0] * G

            def stA(g, c=c):
                qt, f = divmod(g, 8)
                qs = qt * 128
                pr = g % 2
                w = [(se, k_se)]
                if rel_sc[0] is not None:
                    w.append(rel_sc[0])
                for hh in range(2):
                    off = 64 * hh
                    lhsT = q_sb[off:off + 64, f, qs:qs + 128]
                    if qt == 0:
                        mms = [(lhsT, ktail[off:off + 64, f, :], 0, 128),
                               (lhsT, k_sb[off:off + 64, f, 0:128], 128, 128)]
                    else:
                        mms = [(lhsT,
                                k_sb[off:off + 64, f, qs - 128:qs + 128],
                                0, 256)]
                    for i, (lh, rh, cc0, nn) in enumerate(mms):
                        last = (hh == 1 and i == len(mms) - 1)
                        P.op("te", lambda e, hh=hh, lh=lh, rh=rh, cc0=cc0,
                             nn=nn, pr=pr: e.matmul(
                                 ps_sc[hh][:, pr * 256 + cc0:pr * 256 + cc0 + nn],
                                 lh, rh, start=True, stop=True),
                             waits=w if (hh == 0 and i == 0) else (),
                             inc=pe if last else None)
                        w = ()
                smm[g] = pe.n

            def stB(g, c=c):
                qt, f = divmod(g, 8)
                mb = g % 2
                m = self.maskA0B[:] if (c == 0 and qt == 0) else self.maskAB[:]
                w = [(pe, smm[g]), self.tk_consts]
                if rel_masked[mb] is not None:
                    w.append(rel_masked[mb])
                for hh in range(2):
                    P.op("ve", lambda e, mb=mb, hh=hh, m=m:
                         e.tensor_tensor(
                             out=maskedb[:, mb, hh * 256:(hh + 1) * 256],
                             in0=ps_sc[hh][:, mb * 256:mb * 256 + 256],
                             in1=m, op=OP.add),
                         waits=w if hh == 0 else ())
                    w = ()
                P.drain("ve")
                mb3 = maskedb[:, mb, :].rearrange("p (s x) -> p s x", s=2)
                P.op("ve", lambda e, f=f, mb3=mb3: e.tensor_reduce(
                    out=mn_sb[:, 2 * f:2 * f + 2], in_=mb3, axis=AXX,
                    op=OP.max, negate=True))
                P.drain("ve")
                P.op("ve", lambda e, f=f: e.tensor_scalar(
                    out=mn_sb[:, 2 * f:2 * f + 2],
                    in0=mn_sb[:, 2 * f:2 * f + 2],
                    scalar1=0.0, scalar2=None, op0=OP.min, op1=OP.bypass),
                    inc=ve)
                ttrv[g] = ve.n
                rel_sc[0] = (ve, ve.n)

            def stC(g, c=c):
                qt, f = divmod(g, 8)
                mb = g % 2
                w = [(ve, ttrv[g])]
                if rel_p[mb] is not None:
                    w.append(rel_p[mb])
                for hh in range(2):
                    P.op("se", lambda e, mb=mb, hh=hh, f=f: e.activation(
                        out=p_b[:, mb, hh * 256:(hh + 1) * 256],
                        in_=maskedb[:, mb, hh * 256:(hh + 1) * 256],
                        func=AF.Exp,
                        bias=mn_sb[:, 2 * f + hh:2 * f + hh + 1],
                        accum_out=ssum[:, 2 * f + hh:2 * f + hh + 1]),
                        waits=w if hh == 0 else (),
                        inc=se if hh == 1 else None)
                    w = ()
                expv[g] = se.n
                rel_masked[mb] = (se, se.n)

            def stD(g, c=c):
                qt, f = divmod(g, 8)
                mb = g % 2
                P.op("ve", lambda e, f=f: e.tensor_scalar(
                    out=rvv[:, 2 * f:2 * f + 2], in0=ssum[:, 2 * f:2 * f + 2],
                    scalar1=1e-20, scalar2=None, op0=OP.add, op1=OP.bypass),
                    waits=[(se, expv[g])])
                P.drain("ve")
                P.op("ve", lambda e, f=f: e.reciprocal(
                    out=rvv[:, 2 * f:2 * f + 2], in_=rvv[:, 2 * f:2 * f + 2]))
                P.drain("ve")
                w = []
                if rel_pn[mb] is not None:
                    w.append(rel_pn[mb])
                for hh in range(2):
                    P.op("ve", lambda e, mb=mb, hh=hh, f=f: e.tensor_scalar(
                        out=pn_b[:, mb, hh * 256:(hh + 1) * 256],
                        in0=p_b[:, mb, hh * 256:(hh + 1) * 256],
                        scalar1=rvv[:, 2 * f + hh:2 * f + hh + 1],
                        scalar2=None, op0=OP.mult, op1=OP.bypass),
                        waits=w if hh == 0 else (),
                        inc=ve if hh == 1 else None)
                    w = ()
                nrm[g] = ve.n
                rel_p[mb] = (ve, ve.n)

            def stE(g, c=c):
                mb = g % 2
                slot, w = rot_bt.acquire()
                w = list(w) + [(ve, nrm[g])]
                for i in range(4):
                    P.op("te", lambda e, sl=slot, mb=mb, i=i: e.transpose(
                        ps_bt[sl][:, i * 128:(i + 1) * 128],
                        pn_b[:, mb, i * 128:(i + 1) * 128], self.ident[:]),
                        waits=w if i == 0 else (),
                        inc=pe if i == 3 else None)
                    w = ()
                rel_pn[mb] = (pe, pe.n)
                wv = [(pe, pe.n)]
                if rel_pT[mb] is not None:
                    wv.append(rel_pT[mb])
                P.op("se", lambda e, sl=slot, mb=mb: e.copy(
                    out=pT_b[:, mb, :], in_=ps_bt[sl][:]), waits=wv, inc=se)
                evT[g] = se.n
                rot_bt.release(slot, se, se.n)

            def stG(g, c=c):
                qt, f = divmod(g, 8)
                qs = qt * 128
                mb = g % 2
                slot, w = rot_pr.acquire()
                w = list(w) + [(se, evT[g]), (ve, v_ve)]
                for hh in range(2):
                    h = 2 * f + hh
                    for kt in range(2):
                        if qt == 0 and kt == 0:
                            vap = vtail[:, h * 64:(h + 1) * 64]
                        else:
                            vap = v_sb[:, qt - 1 + kt, h * 64:(h + 1) * 64]
                        P.op("te",
                             lambda e, sl=slot, vap=vap, mb=mb, hh=hh, kt=kt:
                             e.matmul(ps_pr[sl][64 * hh:64 * hh + 64, 0:128],
                                      vap,
                                      pT_b[:, mb, (2 * hh + kt) * 128:
                                           (2 * hh + kt + 1) * 128],
                                      start=(kt == 0), stop=(kt == 1)),
                             waits=w, inc=pe if (hh * 2 + kt) == 3 else None)
                        w = ()
                rel_pT[mb] = (pe, pe.n)
                wev = [(pe, pe.n)]
                if locsb_rel[0] is not None:
                    wev.append(locsb_rel[0])
                P.op("se", lambda e, sl=slot, f=f, qs=qs: e.copy(
                    out=local_sb[:, f, qs:qs + 128], in_=ps_pr[sl][:, 0:128]),
                    waits=wev, inc=se)
                rot_pr.release(slot, se, se.n)

            ST = self.p0_st
            if self.p0_attn:
                for i in range(G + 5):
                    if 1 <= i <= G and ST >= 2:
                        stB(i - 1)
                    if i < G:
                        stA(i)
                    if 2 <= i <= G + 1 and ST >= 3:
                        stC(i - 2)
                    if 3 <= i <= G + 2 and ST >= 4:
                        stD(i - 3)
                    if 4 <= i <= G + 3 and ST >= 5:
                        stE(i - 4)
                    if 5 <= i <= G + 4 and ST >= 6:
                        stG(i - 5)
                    if i < G and i % 8 == 7:
                        emit_pool(6)

            self.tk_locsp = st_locsp.dma(
                P, "sy", d["local_sp"][:, :, c * 512:(c + 1) * 512], local_sb[:],
                waits=[(se, se.n)])
            locsb_rel[0] = self.tk_locsp

        emit_pool(128)

        if not self.p0_pool:
            P.wait("sy", self.tk_locsp)
            P.emit(nc, "p0")
            c0.close()
            return
        P.op("ve", lambda e: e.tensor_copy(out=pooledlm[:], in_=ps_pool[0:64, :]),
             waits=[(pp, 128)], inc=ve)
        plm = ve.n
        for g2 in range(2):
            slot, w = rot_bt.acquire()
            w = list(w) + [(ve, plm)]
            for i in range(4):
                fc = g2 * 4 + i
                P.op("te", lambda e, sl=slot, i=i, fc=fc: e.transpose(
                    ps_bt[sl][:, i * 128:i * 128 + 64],
                    pooledlm[0:64, fc * 128:(fc + 1) * 128],
                    self.ident[0:64, 0:64]),
                    waits=w if i == 0 else (), inc=pe if i == 3 else None)
                w = ()
            mmv = pe.n
            for i in range(4):
                fc = g2 * 4 + i
                P.op("se", lambda e, sl=slot, i=i, fc=fc: e.copy(
                    out=self.pooledfm[:, fc, :],
                    in_=ps_bt[sl][:, i * 128:i * 128 + 64]),
                    waits=[(pe, mmv)], inc=se if i == 3 else None)
            rot_bt.release(slot, se, se.n)
        pfm = se.n
        for mf in range(2):
            def ev(slot, mmv, mf=mf):
                P.op("se", lambda e, sl=slot, mf=mf: e.activation(
                    out=self.rkfm[:, mf, :], in_=ps_pr[sl][:, 0:64],
                    func=AF.Identity, bias=self.rkvbk[:, mf:mf + 1]),
                    waits=[(pe, mmv)], inc=se)
                rot_pr.release(slot, se, se.n)
            proj(lambda kk, mf=mf: rkvw[:, kk, mf * 128:(mf + 1) * 128],
                 lambda kk: self.pooledfm[:, kk, :], 8, ev,
                 wait_first=[(se, pfm), self.tk_rkvw], ncols=64)

        def ev_rv(slot, mmv):
            P.op("ve", lambda e, sl=slot: e.tensor_tensor(
                out=self.rvlm[:], in0=ps_pr[sl][0:64, 0:256], in1=self.rvb[:],
                op=OP.add), waits=[(pe, mmv)], inc=ve)
            rot_pr.release(slot, ve, ve.n)
        proj(lambda kk: self.pooledfm[:, kk, :],
             lambda kk: rkvw[:, kk, 256:512], 8, ev_rv, ncols=256, nrows=64)
        self._rv_ve = ve.n

        P.wait("sy", self.tk_locsp)
        P.emit(nc, "p0")
        c0.close()

    def phase1(self):
        nc, d, sm = self.nc, self.d, self.sem
        pe, ve, se = (sm[k] for k in ("pe", "ve", "se"))
        self.ce = ExitStack()
        self.gw_sb = self.ce.enter_context(
            nc.sbuf_tensor("gw_sb", [128, 16, 8, 256], BF16))

        c1 = ExitStack()
        S = lambda n, s, t: c1.enter_context(nc.sbuf_tensor(n, s, t))
        PS = lambda n, s, t: c1.enter_context(nc.psum_tensor(n, s, t))
        localre = S("localre", [128, 2, 8, 512], BF16)
        remote = S("remote", [128, 8, 512], BF16)
        latout = S("latout", [128, 2, 512], BF16)
        mixed = S("mixed", [128, 8, 512], BF16)
        lmaskb = S("lmaskb", [128, 2, 128], F32)
        lp_b = S("lp_b", [128, 2, 128], BF16)
        lpn_b = S("lpn_b", [128, 2, 128], BF16)
        lpT_b = S("lpT_b", [64, 2, 256], BF16)
        lmacc = S("lmacc", [128, 4], F32)
        lmn = S("lmn", [128, 4], F32)
        lssum = S("lssum", [128, 4], F32)
        lrvv = S("lrvv", [128, 4], F32)
        tf_sb = S("tf_sb", [2, 2, 512], BF16)
        d_bf = S("d_bf", [128, 512], BF16)
        s_bf = S("s_bf", [128, 512], BF16)
        pm_sb = S("pm_sb", [128, 512], F32)
        xres = S("xres", [128, 4, 1024], F32)
        xob = S("xob", [128, 1024], F32)
        y0b = S("y0b", [128, 2, 1024], F32)
        stg = S("stg", [128, 2, 2048], BF16)
        ps_gen = [PS(f"ps_g1{i}", [128, 512], F32) for i in range(3)]
        ps_bt = [PS(f"ps_b1{i}", [128, 512], BF16) for i in range(2)]
        ps_out = PS("ps_out1", [128, 1024], F32)

        P = Plan()
        st_gw = self.new_stream("gw", 1)
        for ge4 in range(4):
            self.tk_gw = st_gw.dma(
                P, "ge", self.gw_sb[:, ge4 * 4:(ge4 + 1) * 4],
                d["gate_w"][ge4 * 4:(ge4 + 1) * 4].rearrange(
                    "e (n p) m -> p e n m", p=128))
        st_pcg = self.new_stream("pcg", 2)
        st_pcs = [self.new_stream(f"pcs{i}", 1) for i in range(2)]
        stg_rel = [None] * 2
        pcs_tok = [None] * 2
        pcj = [0]

        def precast(src_ap, dst_ap, npc):
            sl = pcj[0] % 2
            pcj[0] += 1
            w = [stg_rel[sl]] if stg_rel[sl] is not None else []
            tkl = st_pcg.dma(P, "ge",
                             stg[:, sl, :].rearrange("p (n m) -> p n m", n=npc),
                             src_ap, waits=w)
            tks = st_pcs[sl].dma(P, "ge", dst_ap,
                                 stg[:, sl, :].rearrange("p (n m) -> p n m", n=npc),
                                 waits=[tkl])
            stg_rel[sl] = tks
            pcs_tok[sl] = tks
            return tks

        for e_i in range(16):
            precast(R3(d["down_w"][e_i]), d["dwbf_sp"][e_i], 2)
        for e_i in range(16):
            precast(R3(d["up_w"][e_i]), d["uwbf_sp"][e_i], 8)
        self.tk_precast_ud = list(pcs_tok)
        for srcw, dstw in [(d["sgw_t"], d["swg_sp"]), (d["suw_t"], d["swu_sp"])]:
            for pc in range(8):
                precast(R3(srcw)[:, pc:pc + 1, :], dstw[:, pc:pc + 1, :], 1)
        for pc in range(8):
            precast(R3(d["sdw_t"])[:, 2 * pc:2 * pc + 2, :],
                    d["swd_sp"][:, 2 * pc:2 * pc + 2, :], 2)
        self.tk_precast = list(pcs_tok)

        rot_g = Rot(3)
        rot_bt1 = Rot(2)
        rel_lmask = [None, None]
        rel_lp = [None, None]
        rel_lpn = [None, None]
        rel_lpT = [None, None]
        rel_tf = [None, None]
        y0_rel = [None, None]
        psout_rel = None

        st_loc = self.new_stream("lre", 2)
        st_xres = self.new_stream("xres", 4)
        st_y0sp = self.new_stream("y0sp", 2)
        loc_rel = [None, None]
        xres_rel4 = [None] * 4
        xob_rel = [None]
        for c in range(NCH):
            lsl = c % 2
            wl = [self.tk_locsp]
            if loc_rel[lsl] is not None:
                wl.append(loc_rel[lsl])
            ld_loc = st_loc.dma(P, "sy", localre[:, lsl],
                                d["local_sp"][:, :, c * 512:(c + 1) * 512],
                                waits=wl)
            ldx_t = {}
            for mt in range(4):
                w = [xres_rel4[mt]] if xres_rel4[mt] is not None else []
                ldx_t[mt] = st_xres.dma(
                    P, "sy", xres[:, mt, :],
                    d["x_own"][(c * 4 + mt) * 128:(c * 4 + mt + 1) * 128, :],
                    waits=w)

            for qt in range(4):
                gq = 4 * c + qt
                qs = qt * 128
                for fh in range(2):
                    mb = fh
                    slotA, wA = rot_g.acquire()
                    slotB, wB = rot_g.acquire()
                    w = list(wA) + list(wB)
                    for hh in range(2):
                        off = 64 * hh
                        sl_h = slotA if hh == 0 else slotB
                        P.op("te", lambda e, sl=sl_h, fh=fh, hh=hh, off=off,
                             gq=gq: e.matmul(
                                 ps_gen[sl][:, 0:64],
                                 self.rq_sb[off:off + 64, fh,
                                            gq * 128:(gq + 1) * 128],
                                 self.rkfm[off:off + 64, fh, :],
                                 start=True, stop=True),
                             waits=w if hh == 0 else (),
                             inc=pe if hh == 1 else None)
                        w = ()
                    vmm = pe.n
                    wv = [(pe, vmm), self.tk_consts]
                    if rel_lmask[mb] is not None:
                        wv.append(rel_lmask[mb])
                    for hh in range(2):
                        sl_h = slotA if hh == 0 else slotB
                        P.op("ve", lambda e, sl=sl_h, mb=mb, hh=hh, gq=gq:
                             e.tensor_tensor(
                                 out=lmaskb[:, mb, hh * 64:(hh + 1) * 64],
                                 in0=ps_gen[sl][:, 0:64],
                                 in1=self.latmask[:, gq, :], op=OP.add),
                             waits=wv if hh == 0 else ())
                        wv = ()
                    P.drain("ve")
                    lb3 = lmaskb[:, mb, :].rearrange("p (s x) -> p s x", s=2)
                    P.op("ve", lambda e, fh=fh, lb3=lb3: e.tensor_reduce(
                        out=lmn[:, 2 * fh:2 * fh + 2], in_=lb3, axis=AXX,
                        op=OP.max, negate=True))
                    P.drain("ve")
                    P.op("ve", lambda e, fh=fh: e.tensor_scalar(
                        out=lmn[:, 2 * fh:2 * fh + 2],
                        in0=lmn[:, 2 * fh:2 * fh + 2],
                        scalar1=0.0, scalar2=None, op0=OP.min, op1=OP.bypass),
                        inc=ve)
                    vttr = ve.n
                    rot_g.release(slotA, ve, ve.n)
                    rot_g.release(slotB, ve, ve.n)
                    ws = [(ve, vttr)]
                    if rel_lp[mb] is not None:
                        ws.append(rel_lp[mb])
                    for hh in range(2):
                        P.op("se", lambda e, mb=mb, hh=hh, fh=fh: e.activation(
                            out=lp_b[:, mb, hh * 64:(hh + 1) * 64],
                            in_=lmaskb[:, mb, hh * 64:(hh + 1) * 64], func=AF.Exp,
                            bias=lmn[:, 2 * fh + hh:2 * fh + hh + 1],
                            accum_out=lssum[:, 2 * fh + hh:2 * fh + hh + 1]),
                            waits=ws if hh == 0 else (),
                            inc=se if hh == 1 else None)
                        ws = ()
                    vexp = se.n
                    rel_lmask[mb] = (se, se.n)
                    P.op("ve", lambda e, fh=fh: e.tensor_scalar(
                        out=lrvv[:, 2 * fh:2 * fh + 2],
                        in0=lssum[:, 2 * fh:2 * fh + 2],
                        scalar1=1e-20, scalar2=None, op0=OP.add, op1=OP.bypass),
                        waits=[(se, vexp)])
                    P.drain("ve")
                    P.op("ve", lambda e, fh=fh: e.reciprocal(
                        out=lrvv[:, 2 * fh:2 * fh + 2],
                        in_=lrvv[:, 2 * fh:2 * fh + 2]))
                    P.drain("ve")
                    wn = []
                    if rel_lpn[mb] is not None:
                        wn.append(rel_lpn[mb])
                    for hh in range(2):
                        P.op("ve", lambda e, mb=mb, hh=hh, fh=fh: e.tensor_scalar(
                            out=lpn_b[:, mb, hh * 64:(hh + 1) * 64],
                            in0=lp_b[:, mb, hh * 64:(hh + 1) * 64],
                            scalar1=lrvv[:, 2 * fh + hh:2 * fh + hh + 1],
                            scalar2=None, op0=OP.mult, op1=OP.bypass),
                            waits=wn if hh == 0 else (),
                            inc=ve if hh == 1 else None)
                        wn = ()
                    vnrm = ve.n
                    rel_lp[mb] = (ve, ve.n)
                    slot2, w2 = rot_bt1.acquire()
                    w2 = list(w2) + [(ve, vnrm)]
                    for hh in range(2):
                        P.op("te", lambda e, sl=slot2, mb=mb, hh=hh: e.transpose(
                            ps_bt[sl][0:64, hh * 128:(hh + 1) * 128],
                            lpn_b[:, mb, hh * 64:(hh + 1) * 64], self.ident[:]),
                            waits=w2 if hh == 0 else (), inc=pe if hh == 1 else None)
                        w2 = ()
                    rel_lpn[mb] = (pe, pe.n)
                    wvv = [(pe, pe.n)]
                    if rel_lpT[mb] is not None:
                        wvv.append(rel_lpT[mb])
                    P.op("se", lambda e, sl=slot2, mb=mb: e.copy(
                        out=lpT_b[0:64, mb, :], in_=ps_bt[sl][0:64, 0:256]),
                        waits=wvv, inc=se)
                    vevT = se.n
                    rot_bt1.release(slot2, se, se.n)
                    slot3, w3 = rot_g.acquire()
                    w3 = list(w3) + [(se, vevT), (ve, self._rv_ve)]
                    for hh in range(2):
                        h4 = 2 * fh + hh
                        P.op("te", lambda e, sl=slot3, mb=mb, hh=hh, h4=h4:
                             e.matmul(ps_gen[sl][64 * hh:64 * hh + 64, 0:128],
                                      self.rvlm[0:64, h4 * 64:(h4 + 1) * 64],
                                      lpT_b[0:64, mb, hh * 128:(hh + 1) * 128],
                                      start=True, stop=True),
                             waits=w3 if hh == 0 else (),
                             inc=pe if hh == 1 else None)
                        w3 = ()
                    rel_lpT[mb] = (pe, pe.n)
                    P.op("se", lambda e, sl=slot3, fh=fh, qs=qs: e.copy(
                        out=latout[:, fh, qs:qs + 128], in_=ps_gen[sl][:, 0:128]),
                        waits=[(pe, pe.n)], inc=se)
                    rot_g.release(slot3, se, se.n)
            lat_se = se.n

            for of in range(8):
                slot, w = rot_g.acquire()
                w = list(w) + [(se, lat_se), self.tk_routw]
                for lc in range(2):
                    P.op("te", lambda e, sl=slot, of=of, lc=lc: e.matmul(
                        ps_gen[sl][:], self.routw[:, lc, of * 128:(of + 1) * 128],
                        latout[:, lc, :], start=(lc == 0), stop=(lc == 1)),
                        waits=w if lc == 0 else (), inc=pe if lc == 1 else None)
                    w = ()
                vmm = pe.n
                P.op("se", lambda e, sl=slot, of=of: e.activation(
                    out=remote[:, of, :], in_=ps_gen[sl][:], func=AF.Identity,
                    bias=self.routbt[:, of:of + 1]), waits=[(pe, vmm)], inc=se)
                rot_g.release(slot, se, se.n)
            rem_se = se.n

            for f in range(8):
                fb = f % 2
                slot, w = rot_g.acquire()
                w = list(w) + [(se, rem_se), ld_loc, self.tk_gec]
                P.op("te", lambda e, sl=slot, f=f, lsl=lsl: e.matmul(
                    ps_gen[sl][0:2, :], self.glgr[:, 0:2], localre[:, lsl, f, :],
                    start=True, stop=False), waits=w)
                P.op("te", lambda e, sl=slot, f=f: e.matmul(
                    ps_gen[sl][0:2, :], self.glgr[:, 2:4], remote[:, f, :],
                    start=False, stop=True), inc=pe)
                vz = pe.n
                wt = [(pe, vz)]
                if rel_tf[fb] is not None:
                    wt.append(rel_tf[fb])
                P.op("se", lambda e, sl=slot, fb=fb: e.activation(
                    out=tf_sb[0:2, fb, :], in_=ps_gen[sl][0:2, :], func=AF.Tanh,
                    scale=0.5), waits=wt, inc=se)
                vth = se.n
                rot_g.release(slot, se, se.n)
                slot2, w2 = rot_g.acquire()
                P.op("te", lambda e, sl=slot2, fb=fb: e.matmul(
                    ps_gen[sl][:], self.rep2[:], tf_sb[0:2, fb, :],
                    start=True, stop=True),
                    waits=list(w2) + [(se, vth)], inc=pe)
                vtr = pe.n
                rel_tf[fb] = (pe, pe.n)
                P.op("ve", lambda e, f=f, lsl=lsl: e.tensor_tensor(
                    out=d_bf[:], in0=localre[:, lsl, f, :], in1=remote[:, f, :],
                    op=OP.subtract), waits=[(pe, vtr)])
                P.op("ve", lambda e, f=f, lsl=lsl: e.tensor_tensor(
                    out=s_bf[:], in0=localre[:, lsl, f, :], in1=remote[:, f, :],
                    op=OP.add))
                P.drain("ve")
                P.op("ve", lambda e, sl=slot2: e.tensor_tensor(
                    out=pm_sb[:], in0=d_bf[:], in1=ps_gen[sl][:], op=OP.mult),
                    inc=ve)
                rot_g.release(slot2, ve, ve.n)
                P.drain("ve")
                P.op("ve", lambda e, f=f: e.scalar_tensor_tensor(
                    out=mixed[:, f, :], in0=s_bf[:], scalar=0.5, in1=pm_sb[:],
                    op0=OP.mult, op1=OP.add), inc=ve)
                P.drain("ve")
            mix_ve = ve.n
            loc_rel[lsl] = (pe, pe.n)

            for mt in range(4):
                sl = mt % 2
                wxo = [ldx_t[mt], self.tk_consts]
                if xob_rel[0] is not None:
                    wxo.append(xob_rel[0])
                P.op("ve", lambda e, mt=mt: e.tensor_tensor(
                    out=xob[:], in0=xres[:, mt, :], in1=self.outbrep[:],
                    op=OP.add), waits=wxo, inc=ve)
                xres_rel4[mt] = (ve, ve.n)
                vxob = ve.n
                w = [(ve, mix_ve), self.tk_outw]
                if psout_rel is not None:
                    w.append(psout_rel)
                for cc in range(8):
                    for nh in range(2):
                        P.op("te", lambda e, cc=cc, nh=nh, mt=mt: e.matmul(
                            ps_out[:, nh * 512:(nh + 1) * 512],
                            mixed[:, cc, mt * 128:(mt + 1) * 128],
                            self.outw[:, cc, nh * 512:(nh + 1) * 512],
                            start=(cc == 0), stop=(cc == 7)),
                            waits=w if (cc == 0 and nh == 0) else (),
                            inc=pe if (cc == 7 and nh == 1) else None)
                        w = ()
                vo = pe.n
                wy = [(pe, vo), (ve, vxob)]
                if y0_rel[sl] is not None:
                    wy.append(y0_rel[sl])
                P.op("ve", lambda e, sl=sl: e.tensor_tensor(
                    out=y0b[:, sl, :], in0=ps_out[:], in1=xob[:],
                    op=OP.add), waits=wy, inc=ve)
                xob_rel[0] = (ve, ve.n)
                psout_rel = (ve, ve.n)
                vy0 = ve.n
                self.tk_y0sp = st_y0sp.dma(P, "sy", d["y0_sp"][c * 4 + mt],
                                           y0b[:, sl, :], waits=[(ve, vy0)])
                y0_rel[sl] = self.tk_y0sp

        self.tk_y0sp_all = [(s, s.n) for s in st_y0sp.sems]
        P.wait("sy", *self.tk_y0sp_all)
        P.emit(nc, "p1")
        c1.close()

    def phase15(self):
        nc, d, sm = self.nc, self.d, self.sem
        pe, ve, se = (sm[k] for k in ("pe", "ve", "se"))
        c2 = ExitStack()
        S = lambda n, s, t: c2.enter_context(nc.sbuf_tensor(n, s, t))
        PS = lambda n, s, t: c2.enter_context(nc.psum_tensor(n, s, t))
        y0r = S("y0r", [128, 4, 1024], F32)
        bnst = S("bnst", [128, 4, 12], F32)
        mv = S("mv", [128, 4, 2], F32)
        lnv = S("lnv", [128, 4], F32)
        rstd = S("rstd", [128, 4], F32)
        nmr = S("nmr", [128, 4], F32)
        y1f = S("y1f", [128, 4, 1024], F32)
        y1bf = S("y1bf", [128, 2, 1024], BF16)
        y1stage = S("y1stage", [128, 2, 4, 128], BF16)
        sco_all = S("sco_all", [128, 16, 16], F32)
        enl_all = S("enl_all", [128, 16, 16], F32)
        scor_all = S("scor_all", [128, 16, 16], F32)
        sb_all = S("sb_all", [128, 16, 16], F32)
        prs_all = S("prs_all", [128, 16, 6, 4], F32)
        gsum_all = S("gsum_all", [128, 16, 4], F32)
        cge_all = S("cge_all", [128, 16, 4], F32)
        cgt_all = S("cgt_all", [128, 16, 4], F32)
        grank_all = S("grank_all", [128, 16, 4], F32)
        gmask_all = S("gmask_all", [128, 16, 4], F32)
        maskd_all = S("maskd_all", [128, 16, 16], F32)
        maskd0_all = S("maskd0_all", [128, 16, 16], F32)
        mk_all = S("mk_all", [128, 16], F32)
        tm_all = S("tm_all", [128, 16], F32)
        eq_all = S("eq_all", [128, 16, 16], F32)
        keep_all = S("keep_all", [128, 16, 16], F32)
        wts_all = S("wts_all", [128, 16, 16], F32)
        wsum_all = S("wsum_all", [128, 16], F32)
        comb_all = S("comb_all", [128, 16, 16], BF16)
        ps_bt = [PS(f"ps_b2{i}", [128, 512], BF16) for i in range(2)]
        ps_lg = [PS(f"ps_lg{i}", [128, 16], F32) for i in range(2)]
        ps_ct = [PS(f"ps_ct{i}", [16, 512], BF16) for i in range(2)]

        P = Plan()
        rot_bt2 = Rot(2)
        y0r_rel = [None] * 4
        y1f_rel = [None] * 4
        lg_rel = [None, None]
        st_y0r = self.new_stream("y0r", 8)
        st_y1sp = self.new_stream("y1sp", 8)
        st_y1fmsp = self.new_stream("yfsp", 4)

        vmv_a = [0] * 16
        vrs_a = [0] * 16
        vnm_a = [0] * 16
        vy1b_a = [0] * 16
        mv_rel = [None] * 4
        nmr_rel = [None] * 4
        y1bf_rel = [None, None]

        def L1(t):
            sl = t % 4
            w = list(self.tk_y0sp_all)
            if y0r_rel[sl] is not None:
                w.append(y0r_rel[sl])
            ldt0 = st_y0r.dma(P, "sy", y0r[:, sl, 0:512],
                              d["y0_sp"][t][:, 0:512], waits=w)
            ldt1 = st_y0r.dma(P, "sy", y0r[:, sl, 512:1024],
                              d["y0_sp"][t][:, 512:1024], waits=w)
            P.op("ve", lambda e, sl=sl: e.bn_stats(out=bnst[:, sl, 0:6],
                                                   in_=y0r[:, sl, 0:512]),
                 waits=[ldt0])
            P.op("ve", lambda e, sl=sl: e.bn_stats(out=bnst[:, sl, 6:12],
                                                   in_=y0r[:, sl, 512:1024]),
                 waits=[ldt1])
            P.drain("ve")
            wagg = [mv_rel[sl]] if mv_rel[sl] is not None else []
            P.op("ve", lambda e, sl=sl: e.bn_aggr(out=mv[:, sl, :],
                                                  in_=bnst[:, sl, :]),
                 waits=wagg, inc=ve)
            vmv_a[t] = ve.n

        def L2(t):
            sl = t % 4
            P.op("se", lambda e, sl=sl: e.activation(
                out=lnv[:, sl:sl + 1], in_=mv[:, sl, 1:2],
                func=AF.Ln, bias=self.epsv[:, 0:1]),
                waits=[(ve, vmv_a[t]), self.tk_eps])
            P.drain("se")
            P.op("se", lambda e, sl=sl: e.activation(
                out=rstd[:, sl:sl + 1], in_=lnv[:, sl:sl + 1],
                func=AF.Exp, scale=-0.5), inc=se)
            vrs_a[t] = se.n
            mv_rel[sl] = (se, se.n)

        def L3(t):
            sl = t % 4
            w = [(se, vrs_a[t])]
            if nmr_rel[sl] is not None:
                w.append(nmr_rel[sl])
            P.op("ve", lambda e, sl=sl: e.tensor_scalar(
                out=nmr[:, sl:sl + 1], in0=mv[:, sl, 0:1],
                scalar1=rstd[:, sl:sl + 1], scalar2=-1.0,
                op0=OP.mult, op1=OP.mult), waits=w, inc=ve)
            vnm_a[t] = ve.n

        def L4(t):
            sl = t % 4
            sb2 = t % 2
            wy1 = [(ve, vnm_a[t])]
            if y1f_rel[sl] is not None:
                wy1 += y1f_rel[sl]
            P.op("se", lambda e, sl=sl: e.activation(
                out=y1f[:, sl, :], in_=y0r[:, sl, :], func=AF.Identity,
                scale=rstd[:, sl:sl + 1], bias=nmr[:, sl:sl + 1]),
                waits=wy1, inc=se)
            y0r_rel[sl] = (se, se.n)
            nmr_rel[sl] = (se, se.n)
            P.drain("se")
            wbf = [y1bf_rel[sb2]] if y1bf_rel[sb2] is not None else []
            P.op("se", lambda e, sl=sl, sb2=sb2: e.copy(
                out=y1bf[:, sb2, :], in_=y1f[:, sl, :]), waits=wbf, inc=se)
            vy1b_a[t] = se.n
            tks0 = st_y1sp.dma(P, "sy", d["y1_sp"][t][:, 0:512],
                               y1f[:, sl, 0:512], waits=[(se, vy1b_a[t])])
            tks1 = st_y1sp.dma(P, "sy", d["y1_sp"][t][:, 512:1024],
                               y1f[:, sl, 512:1024], waits=[(se, vy1b_a[t])])
            y1f_rel[sl] = [tks0, tks1]

        def L5(t):
            sb2 = t % 2
            yst_w = [(s, s.n) for s in st_y1fmsp.sems] if t > 0 else []
            if self.p15_mode < 1:
                return
            for g2 in range(2):
                slot, w2 = rot_bt2.acquire()
                w2 = list(w2) + [(se, vy1b_a[t])] + yst_w
                for i in range(4):
                    fc = g2 * 4 + i
                    P.op("te", lambda e, sl2=slot, i=i, fc=fc, sb2=sb2:
                         e.transpose(
                             ps_bt[sl2][:, i * 128:(i + 1) * 128],
                             y1bf[:, sb2, fc * 128:(fc + 1) * 128],
                             self.ident[:]),
                         waits=w2 if i == 0 else (), inc=pe if i == 3 else None)
                    w2 = ()
                vtr = pe.n
                P.op("se", lambda e, sl2=slot, g2=g2: e.copy(
                    out=y1stage[:, g2].rearrange("p n m -> p (n m)"),
                    in_=ps_bt[sl2][:]), waits=[(pe, vtr)], inc=se)
                rot_bt2.release(slot, se, se.n)
            y1bf_rel[sb2] = (pe, pe.n)
            vstage = se.n
            st_y1fmsp.dma(P, "sy", d["y1fm_sp"][:, 0:4, t * 128:(t + 1) * 128],
                          y1stage[:, 0], waits=[(se, vstage)])
            self.tk_yfsp = st_y1fmsp.dma(
                P, "sy", d["y1fm_sp"][:, 4:8, t * 128:(t + 1) * 128],
                y1stage[:, 1], waits=[(se, vstage)])
            if self.p15_mode < 2:
                return
            slg = t % 2
            wlg = [(se, vstage), self.tk_routerw]
            if lg_rel[slg] is not None:
                wlg.append(lg_rel[slg])
            for cc in range(8):
                P.op("te", lambda e, slg=slg, cc=cc: e.matmul(
                    ps_lg[slg][:], y1stage[:, cc // 4, cc % 4, :],
                    self.routerw[:, cc, :], start=(cc == 0), stop=(cc == 7)),
                    waits=wlg if cc == 0 else (),
                    inc=pe if cc == 7 else None)
            vlg = pe.n
            P.op("se", lambda e, slg=slg, t=t: e.copy(
                out=sco_all[:, t, :], in_=ps_lg[slg][:]),
                waits=[(pe, vlg)], inc=se)
            lg_rel[slg] = (se, se.n)

        for i in range(16 + 4):
            if i < 16:
                L1(i)
            if 0 <= i - 1 < 16:
                L2(i - 1)
            if 0 <= i - 2 < 16:
                L3(i - 2)
            if 0 <= i - 3 < 16:
                L4(i - 3)
            if 0 <= i - 4 < 16:
                L5(i - 4)

        if self.p15_mode >= 3:
            v_sco = se.n
            P.drain("se")
            P.op("se", lambda e: e.activation(
                out=enl_all[:].rearrange("p a b -> p (a b)"),
                in_=sco_all[:].rearrange("p a b -> p (a b)"),
                func=AF.Exp, scale=-1.0), inc=se)
            vexp = se.n
            P.op("ve", lambda e: e.tensor_scalar(
                out=scor_all[:], in0=enl_all[:], scalar1=1.0, scalar2=None,
                op0=OP.add, op1=OP.bypass), waits=[(se, vexp), self.tk_consts])
            P.drain("ve")
            P.op("ve", lambda e: e.reciprocal(
                out=scor_all[:].rearrange("p a b -> p (a b)"),
                in_=scor_all[:].rearrange("p a b -> p (a b)")))
            P.drain("ve")
            corr_bc = self.corr[:].unsqueeze(1).broadcast_to([128, 16, 16])
            P.op("ve", lambda e, corr_bc=corr_bc: e.tensor_tensor(
                out=sb_all[:], in0=scor_all[:], in1=corr_bc, op=OP.add))
            P.drain("ve")
            sb4 = sb_all[:].rearrange("p t (g e) -> p t g e", e=4)
            for pi, (i_, j_) in enumerate([(0, 1), (0, 2), (0, 3), (1, 2),
                                           (1, 3), (2, 3)]):
                P.op("ve", lambda e, pi=pi, i_=i_, j_=j_, sb4=sb4:
                     e.tensor_tensor(
                         out=prs_all[:, :, pi, :], in0=sb4[:, :, :, i_],
                         in1=sb4[:, :, :, j_], op=OP.add))
            P.drain("ve")
            P.op("ve", lambda e: e.tensor_tensor(
                out=gsum_all[:], in0=prs_all[:, :, 0, :],
                in1=prs_all[:, :, 1, :], op=OP.max))
            P.drain("ve")
            for pi in range(2, 6):
                P.op("ve", lambda e, pi=pi: e.tensor_tensor(
                    out=gsum_all[:], in0=gsum_all[:], in1=prs_all[:, :, pi, :],
                    op=OP.max))
                P.drain("ve")
            for gi in range(4):
                gs_bc = gsum_all[:, :, gi:gi + 1].broadcast_to([128, 16, 4])
                P.op("ve", lambda e, gs_bc=gs_bc: e.tensor_tensor(
                    out=cge_all[:], in0=gsum_all[:], in1=gs_bc, op=OP.is_ge))
                P.op("ve", lambda e, gs_bc=gs_bc: e.tensor_tensor(
                    out=cgt_all[:], in0=gsum_all[:], in1=gs_bc, op=OP.is_gt))
                P.drain("ve")
                if gi == 0:
                    P.op("ve", lambda e: e.tensor_reduce(
                        out=grank_all[:, :, 0:1], in_=cgt_all[:, :, 1:4],
                        axis=AXX, op=OP.add))
                elif gi == 3:
                    P.op("ve", lambda e: e.tensor_reduce(
                        out=grank_all[:, :, 3:4], in_=cge_all[:, :, 0:3],
                        axis=AXX, op=OP.add))
                else:
                    P.op("ve", lambda e, gi=gi: e.tensor_reduce(
                        out=grank_all[:, :, gi:gi + 1], in_=cge_all[:, :, 0:gi],
                        axis=AXX, op=OP.add))
                    P.op("ve", lambda e, gi=gi: e.tensor_reduce(
                        out=tm_all[:], in_=cgt_all[:, :, gi + 1:4],
                        axis=AXX, op=OP.add))
                    P.drain("ve")
                    P.op("ve", lambda e, gi=gi: e.tensor_tensor(
                        out=grank_all[:, :, gi:gi + 1],
                        in0=grank_all[:, :, gi:gi + 1],
                        in1=tm_all[:].unsqueeze(2), op=OP.add))
                P.drain("ve")
            P.op("ve", lambda e: e.tensor_scalar(
                out=gmask_all[:], in0=grank_all[:], scalar1=1.5, scalar2=None,
                op0=OP.is_le, op1=OP.bypass))
            P.drain("ve")
            for g in range(4):
                gm_bc = gmask_all[:, :, g:g + 1].broadcast_to([128, 16, 4])
                P.op("ve", lambda e, g=g, gm_bc=gm_bc: e.tensor_tensor(
                    out=maskd_all[:, :, g * 4:(g + 1) * 4],
                    in0=sb_all[:, :, g * 4:(g + 1) * 4], in1=gm_bc,
                    op=OP.mult))
            P.drain("ve")
            P.op("ve", lambda e: e.tensor_scalar(
                out=gmask_all[:], in0=gmask_all[:], scalar1=1e5, scalar2=-1e5,
                op0=OP.mult, op1=OP.add))
            P.drain("ve")
            for g in range(4):
                gm_bc = gmask_all[:, :, g:g + 1].broadcast_to([128, 16, 4])
                P.op("ve", lambda e, g=g, gm_bc=gm_bc: e.tensor_tensor(
                    out=maskd_all[:, :, g * 4:(g + 1) * 4],
                    in0=maskd_all[:, :, g * 4:(g + 1) * 4], in1=gm_bc,
                    op=OP.add))
            P.drain("ve")
            P.op("ve", lambda e: e.tensor_copy(
                out=maskd0_all[:].rearrange("p a b -> p (a b)"),
                in_=maskd_all[:].rearrange("p a b -> p (a b)")))
            P.drain("ve")
            mk_bc = mk_all[:].unsqueeze(2).broadcast_to([128, 16, 16])
            for kk in range(4):
                P.op("ve", lambda e: e.tensor_reduce(
                    out=mk_all[:], in_=maskd_all[:], axis=AXX, op=OP.max))
                P.drain("ve")
                if kk < 3:
                    P.op("ve", lambda e, mk_bc=mk_bc: e.tensor_tensor(
                        out=eq_all[:], in0=maskd_all[:], in1=mk_bc,
                        op=OP.is_ge))
                    P.drain("ve")
                    P.op("ve", lambda e: e.scalar_tensor_tensor(
                        out=maskd_all[:], in0=eq_all[:], scalar=-3e5,
                        in1=maskd_all[:], op0=OP.mult, op1=OP.add))
                    P.drain("ve")
            P.op("ve", lambda e, mk_bc=mk_bc: e.tensor_tensor(
                out=keep_all[:], in0=maskd0_all[:], in1=mk_bc, op=OP.is_ge))
            P.drain("ve")
            P.op("ve", lambda e: e.tensor_tensor(
                out=wts_all[:], in0=keep_all[:], in1=scor_all[:], op=OP.mult))
            P.drain("ve")
            P.op("ve", lambda e: e.tensor_reduce(
                out=wsum_all[:], in_=wts_all[:], axis=AXX, op=OP.add))
            P.drain("ve")
            P.op("ve", lambda e: e.tensor_scalar(
                out=wsum_all[:], in0=wsum_all[:], scalar1=1e-20, scalar2=None,
                op0=OP.add, op1=OP.bypass))
            P.drain("ve")
            P.op("ve", lambda e: e.reciprocal(out=wsum_all[:], in_=wsum_all[:]))
            P.drain("ve")
            ws_bc = wsum_all[:].unsqueeze(2).broadcast_to([128, 16, 16])
            P.op("ve", lambda e, ws_bc=ws_bc: e.tensor_tensor(
                out=comb_all[:], in0=wts_all[:], in1=ws_bc, op=OP.mult),
                inc=ve)
            vcomb = ve.n
            ct_rel = [None, None]
            for q in range(4):
                slc = q % 2
                wct = [(ve, vcomb)]
                if ct_rel[slc] is not None:
                    wct.append(ct_rel[slc])
                for i in range(4):
                    P.op("te", lambda e, slc=slc, q=q, i=i: e.transpose(
                        ps_ct[slc][0:16, i * 128:(i + 1) * 128],
                        comb_all[:, q * 4 + i, :], self.ident[:]),
                        waits=wct if i == 0 else (),
                        inc=pe if i == 3 else None)
                vct = pe.n
                P.op("se", lambda e, slc=slc, q=q: e.copy(
                    out=self.combT[0:16, q * 512:(q + 1) * 512],
                    in_=ps_ct[slc][0:16, :]), waits=[(pe, vct)], inc=se)
                ct_rel[slc] = (se, se.n)

        self.tk_y1sp_all = [(s, s.n) for s in st_y1sp.sems]
        self.tk_yfsp_all = [(s, s.n) for s in st_y1fmsp.sems]
        P.wait("sy", *(self.tk_y1sp_all + self.tk_yfsp_all))
        P.emit(nc, "p15")
        self._p15_comb_se = se.n
        c2.close()

    def phase2(self):
        nc, d, sm = self.nc, self.d, self.sem
        pe, ve, se = (sm[k] for k in ("pe", "ve", "se"))
        c3 = ExitStack()
        S = lambda n, s, t: c3.enter_context(nc.sbuf_tensor(n, s, t))
        PS = lambda n, s, t: c3.enter_context(nc.psum_tensor(n, s, t))
        y1s = S("y1s", [128, 2, 8, 256], BF16)
        uwb = S("uwb", [128, 6, 8, 256], BF16)
        dwb = S("dwb", [128, 6, 2, 1024], BF16)
        sg_sb = S("sg_sb", [128, 2, 512], BF16)
        h_sb = S("h_sb", [128, 512], BF16)
        hwst = S("hwst", [128, 2, 512], BF16)
        y1r = S("y1r", [128, 2, 1024], F32)
        ry = S("ry", [128, 2, 1024], F32)
        ps_rt = [PS(f"ps_rt{i}", [128, 1024], F32) for i in range(2)]
        ps_g = PS("ps_g2", [128, 512], F32)
        ps_u = [PS(f"ps_u2{i}", [128, 512], F32) for i in range(2)]
        ps_wb = PS("ps_wb", [128, 256], F32)
        wb_sb = S("wb_sb", [128, 2, 256], F32)

        P = Plan()
        st_y1s = self.new_stream("y1s2", 4)
        st_uw = self.new_stream("uw", 6)
        st_dw = self.new_stream("dw", 6)
        st_y1r = self.new_stream("y1r2", 2)
        st_rysp = self.new_stream("rysp", 2)
        y1s_rel = [None, None]
        uwb_rel = [None] * 6
        dwb_rel = [None] * 6
        y1r_rel = [None, None]
        ry_rel = [None, None]
        wb_rel = [None, None]
        wb_ps_rel = [None]
        psu_rel = [None, None]
        rt_rel = [None, None]
        silu_prev = 0
        h_prev = 0
        h_prev2 = [0, 0]
        prev_down_pe = 0

        def emit_y1s_load(tb):
            sl = tb % 2
            w = list(self.tk_yfsp_all) + list(self.tk_precast)
            if y1s_rel[sl] is not None:
                w.append(y1s_rel[sl])
            a = st_y1s.dma(P, "sy", y1s[:, sl, 0:4, :],
                           d["y1fm_sp"][:, 0:4, tb * 256:(tb + 1) * 256],
                           waits=w)
            b = st_y1s.dma(P, "sy", y1s[:, sl, 4:8, :],
                           d["y1fm_sp"][:, 4:8, tb * 256:(tb + 1) * 256],
                           waits=w)
            return [a, b]

        ldy_toks = {0: emit_y1s_load(0)}
        for tb in range(8):
            sl = tb % 2
            ldy_a, ldy_b = ldy_toks[tb]
            if tb + 1 < 8:
                ldy_toks[tb + 1] = emit_y1s_load(tb + 1)
            hw_ve = [0] * 16
            dwld = [0] * 16
            uwld = [0] * 16
            down_pe = [0] * 17
            for e_i in range(17):
                if e_i < 16:
                    su = (tb * 16 + e_i) % 6
                    wu = [uwb_rel[su]] if uwb_rel[su] is not None else []
                    uwld[e_i] = st_uw.dma(P, "sy", uwb[:, su],
                                          d["uwbf_sp"][e_i],
                                          waits=wu + self.tk_precast)
                    wd = [dwb_rel[su]] if dwb_rel[su] is not None else []
                    dwld[e_i] = st_dw.dma(P, "ge", dwb[:, su], d["dwbf_sp"][e_i],
                                          waits=wd + self.tk_precast)
                    sw = e_i % 2
                    if self.p2_mode >= 1:
                        ww = [(se, self._p15_comb_se), self.tk_gec]
                        if wb_ps_rel[0] is not None:
                            ww.append(wb_ps_rel[0])
                        P.op("te", lambda e, e_i=e_i, tb=tb: e.matmul(
                            ps_wb[:], self.ue16[:, e_i * 128:(e_i + 1) * 128],
                            self.combT[0:16, tb * 256:(tb + 1) * 256],
                            start=True, stop=True), waits=ww, inc=pe)
                        vwb = pe.n
                        wwc = [(pe, vwb)] + list(self.tk_precast)
                        if wb_rel[sw] is not None:
                            wwc.append(wb_rel[sw])
                        P.op("se", lambda e, sw=sw: e.copy(
                            out=wb_sb[:, sw, :], in_=ps_wb[:]), waits=wwc,
                            inc=se)
                        wb_ps_rel[0] = (se, se.n)
                    if self.p2_mode < 2:
                        continue
                    wg = [self.tk_gw, ldy_a, ldy_b, (se, silu_prev)]
                    for mh in range(2):
                        for cc in range(8):
                            P.op("te", lambda e, e_i=e_i, cc=cc, mh=mh, sl=sl:
                                 e.matmul(ps_g[:, mh * 256:(mh + 1) * 256],
                                          self.gw_sb[:, e_i, cc,
                                                     mh * 128:(mh + 1) * 128],
                                          y1s[:, sl, cc, :],
                                          start=(cc == 0), stop=(cc == 7)),
                                 waits=wg if (cc == 0 and mh == 0) else (),
                                 inc=pe if (cc == 7 and mh == 1) else None)
                            wg = ()
                    vg = pe.n
                    su2 = e_i % 2
                    wu2 = [uwld[e_i]]
                    if psu_rel[su2] is not None:
                        wu2.append(psu_rel[su2])
                    for mh in range(2):
                        for cc in range(8):
                            P.op("te", lambda e, su=su, su2=su2, cc=cc, mh=mh,
                                 sl=sl:
                                 e.matmul(ps_u[su2][:, mh * 256:(mh + 1) * 256],
                                          uwb[:, su, cc, mh * 128:(mh + 1) * 128],
                                          y1s[:, sl, cc, :],
                                          start=(cc == 0), stop=(cc == 7)),
                                 waits=wu2 if (cc == 0 and mh == 0) else (),
                                 inc=pe if (cc == 7 and mh == 1) else None)
                            wu2 = ()
                    vu = pe.n
                    uwb_rel[su] = (pe, pe.n)
                if e_i >= 1 and self.p2_mode >= 3:
                    ep = e_i - 1
                    sd = (tb * 16 + ep) % 6
                    wmm = [(ve, hw_ve[ep]), dwld[ep]]
                    if ep == 0:
                        for sr in range(2):
                            if rt_rel[sr] is not None:
                                wmm.append(rt_rel[sr])
                    for mt in range(2):
                        for kc in range(2):
                            for nh in range(2):
                                first = (ep == 0 and kc == 0)
                                last = (ep == 15 and kc == 1)
                                P.op("te", lambda e, sd=sd, ep=ep, mt=mt, kc=kc,
                                     nh=nh, first=first, last=last: e.matmul(
                                         ps_rt[mt][:, nh * 512:(nh + 1) * 512],
                                         hwst[:, ep % 2, kc * 256 + mt * 128:
                                              kc * 256 + mt * 128 + 128],
                                         dwb[:, sd, kc, nh * 512:(nh + 1) * 512],
                                         start=first, stop=last),
                                     waits=wmm if (mt == 0 and kc == 0 and nh == 0)
                                     else (),
                                     inc=pe if (mt == 1 and kc == 1 and nh == 1)
                                     else None)
                                wmm = ()
                    down_pe[ep] = pe.n
                    dwb_rel[sd] = (pe, pe.n)
                if e_i < 16:
                    sgsl = e_i % 2
                    P.op("se", lambda e, sgsl=sgsl: e.activation(
                        out=sg_sb[:, sgsl, :], in_=ps_g[:], func=AF.Silu),
                        waits=[(pe, vg), (ve, h_prev2[sgsl])], inc=se)
                    silu_prev = se.n
                    P.op("ve", lambda e, sgsl=sgsl, su2=su2: e.tensor_tensor(
                        out=h_sb[:], in0=sg_sb[:, sgsl, :], in1=ps_u[su2][:],
                        op=OP.mult),
                        waits=[(se, silu_prev), (pe, vu)], inc=ve)
                    h_prev = ve.n
                    h_prev2[sgsl] = ve.n
                    psu_rel[su2] = (ve, ve.n)
                    P.drain("ve")
                    sw = e_i % 2
                    wb3 = wb_sb[:, sw, :].unsqueeze(1).broadcast_to([128, 2, 256])
                    whw = [(pe, prev_down_pe), (se, silu_prev)]
                    P.op("ve", lambda e, e_i=e_i, wb3=wb3: e.tensor_tensor(
                        out=hwst[:, e_i % 2, :].rearrange("p (n m) -> p n m", n=2),
                        in0=h_sb[:].rearrange("p (n m) -> p n m", n=2),
                        in1=wb3, op=OP.mult), waits=whw, inc=ve)
                    hw_ve[e_i] = ve.n
                    wb_rel[sw] = (ve, ve.n)
                    P.drain("ve")
                    prev_down_pe = down_pe[e_i - 1] if e_i >= 1 else prev_down_pe
            for mt in (range(2) if self.p2_mode >= 4 else []):
                sl2 = mt % 2
                wy = list(self.tk_y1sp_all)
                if y1r_rel[sl2] is not None:
                    wy.append(y1r_rel[sl2])
                ldy1 = st_y1r.dma(P, "sy", y1r[:, sl2, :],
                                  d["y1_sp"][tb * 2 + mt], waits=wy)
                wv = [(pe, down_pe[15]), ldy1]
                if ry_rel[sl2] is not None:
                    wv.append(ry_rel[sl2])
                P.op("ve", lambda e, mt=mt, sl2=sl2: e.tensor_tensor(
                    out=ry[:, sl2, :], in0=ps_rt[mt][:], in1=y1r[:, sl2, :],
                    op=OP.add), waits=wv, inc=ve)
                y1r_rel[sl2] = (ve, ve.n)
                rt_rel[mt] = (ve, ve.n)
                vry = ve.n
                tk = st_rysp.dma(P, "sy", d["routedy_sp"][tb * 2 + mt],
                                 ry[:, sl2, :], waits=[(ve, vry)])
                ry_rel[sl2] = tk
            y1s_rel[sl] = (pe, down_pe[15] if self.p2_mode >= 3 else pe.n)

        self.tk_rysp_all = [(s, s.n) for s in st_rysp.sems]
        P.wait("sy", *self.tk_rysp_all)
        P.emit(nc, "p2")
        c3.close()
        self.ce.close()
        self.ca.close()
        self.ce = None
        self.ca = None

    def phase3(self):
        nc, d, sm = self.nc, self.d, self.sem
        pe, ve, se = (sm[k] for k in ("pe", "ve", "se"))
        c4 = ExitStack()
        S = lambda n, s, t: c4.enter_context(nc.sbuf_tensor(n, s, t))
        PS = lambda n, s, t: c4.enter_context(nc.psum_tensor(n, s, t))
        sgw = S("sgw", [128, 8, 2048], BF16)
        suw = S("suw", [128, 8, 2048], BF16)
        sdw = S("sdw", [128, 16, 1024], BF16)
        y1s = S("y1s3", [128, 3, 8, 256], BF16)
        sgs = S("sgs", [128, 2, 256], BF16)
        h_sb = S("h_sb3", [128, 16, 256], BF16)
        ryr = S("ryr", [128, 4, 1024], F32)
        y2p = S("y2p", [128, 2, 1024], F32)
        bnst = S("bnst5", [128, 2, 12], F32)
        mv = S("mv5", [128, 2, 2], F32)
        lnv = S("lnv5", [128, 2], F32)
        rstd = S("rstd5", [128, 2], F32)
        nmr = S("nmr5", [128, 2], F32)
        ob = S("ob5", [128, 2, 1024], F32)
        ps_d = [PS(f"ps_d3{i}", [128, 1024], F32) for i in range(2)]
        ps_g = [PS(f"ps_g3{i}", [128, 512], F32) for i in range(2)]
        ps_u = [PS(f"ps_u3{i}", [128, 512], F32) for i in range(2)]

        P = Plan()
        st_sw = self.new_stream("ssw", 6)
        for qq in range(4):
            st_sw.dma(P, "sy", sgw[:, qq * 2:qq * 2 + 2, :],
                      d["swg_sp"][:, qq * 2:qq * 2 + 2, :],
                      waits=self.tk_precast)
        for qq in range(4):
            st_sw.dma(P, "sy", suw[:, qq * 2:qq * 2 + 2, :],
                      d["swu_sp"][:, qq * 2:qq * 2 + 2, :],
                      waits=self.tk_precast)
        for qq in range(4):
            st_sw.dma(P, "sy", sdw[:, qq * 4:qq * 4 + 4, :],
                      d["swd_sp"][:, qq * 4:qq * 4 + 4, :],
                      waits=self.tk_precast)
        sw_all = [(s, s.n) for s in st_sw.sems]
        ldsg = sw_all
        ldsu = sw_all
        ldsd = sw_all
        st_y1s3 = self.new_stream("y1s3", 4)
        st_ryr = self.new_stream("ryr", 8)
        st_y2sp = self.new_stream("y2sp", 2)
        st_out = self.new_stream("outs", 2)
        ob_rel = [None, None]
        rot_g = Rot(2)
        rot_u = Rot(2)
        y1s_rel = [None] * 3
        ryr_rel = [None] * 4
        y2p_rel = [None, None]
        psd_rel = [None, None]
        prev_down_pe = 0
        h_prev3 = [0, 0]

        def emit_y1s3_load(tb):
            sl = tb % 3
            w = list(self.tk_yfsp_all)
            if y1s_rel[sl] is not None:
                w.append(y1s_rel[sl])
            a = st_y1s3.dma(P, "sy", y1s[:, sl, 0:4, :],
                            d["y1fm_sp"][:, 0:4, tb * 256:(tb + 1) * 256],
                            waits=w)
            b = st_y1s3.dma(P, "sy", y1s[:, sl, 4:8, :],
                            d["y1fm_sp"][:, 4:8, tb * 256:(tb + 1) * 256],
                            waits=w)
            return [a, b]

        ldy3_toks = {0: emit_y1s3_load(0)}
        for tb in range(8):
            sl = tb % 3
            ldy = ldy3_toks[tb]
            if tb + 1 < 8:
                ldy3_toks[tb + 1] = emit_y1s3_load(tb + 1)
            h_ve = [0] * 16
            for mh in range(16):
                slg, wg = rot_g.acquire()
                wg = list(wg) + ldsg + ldy
                for cc in range(8):
                    P.op("te", lambda e, sl2=slg, mh=mh, cc=cc, sl=sl: e.matmul(
                        ps_g[sl2][:, 0:256], sgw[:, cc, mh * 128:(mh + 1) * 128],
                        y1s[:, sl, cc, :], start=(cc == 0), stop=(cc == 7)),
                        waits=wg if cc == 0 else (), inc=pe if cc == 7 else None)
                    wg = ()
                vg = pe.n
                slu, wu = rot_u.acquire()
                wu = list(wu) + ldsu
                for cc in range(8):
                    P.op("te", lambda e, sl2=slu, mh=mh, cc=cc, sl=sl: e.matmul(
                        ps_u[sl2][:, 0:256], suw[:, cc, mh * 128:(mh + 1) * 128],
                        y1s[:, sl, cc, :], start=(cc == 0), stop=(cc == 7)),
                        waits=wu if cc == 0 else (), inc=pe if cc == 7 else None)
                    wu = ()
                vu = pe.n
                sgsl = mh % 2
                P.op("se", lambda e, sl2=slg, sgsl=sgsl: e.activation(
                    out=sgs[:, sgsl, :], in_=ps_g[sl2][:, 0:256], func=AF.Silu),
                    waits=[(pe, vg), (ve, h_prev3[sgsl])], inc=se)
                vsg = se.n
                rot_g.release(slg, se, se.n)
                P.op("ve", lambda e, sl2=slu, mh=mh, sgsl=sgsl: e.tensor_tensor(
                    out=h_sb[:, mh, :], in0=sgs[:, sgsl, :],
                    in1=ps_u[sl2][:, 0:256], op=OP.mult),
                    waits=[(se, vsg), (pe, vu), (pe, prev_down_pe)], inc=ve)
                h_ve[mh] = ve.n
                h_prev3[sgsl] = ve.n
                rot_u.release(slu, ve, ve.n)
            y1s_rel[sl] = (pe, pe.n)
            for mt in range(2):
                sld = (tb * 2 + mt) % 2
                wmm = [(ve, h_ve[15])] + ldsd
                if psd_rel[sld] is not None:
                    wmm.append(psd_rel[sld])
                for kc in range(16):
                    for nh in range(2):
                        first = (kc == 0)
                        last = (kc == 15)
                        P.op("te", lambda e, kc=kc, nh=nh, mt=mt, first=first,
                             last=last, sld=sld: e.matmul(
                            ps_d[sld][:, nh * 512:(nh + 1) * 512],
                            h_sb[:, kc, mt * 128:(mt + 1) * 128],
                            sdw[:, kc, nh * 512:(nh + 1) * 512],
                            start=first, stop=last),
                            waits=wmm if first else (), inc=pe if last else None)
                        wmm = ()
                vd = pe.n
                prev_down_pe = vd
                sl2 = mt % 2
                sl4 = (tb * 2 + mt) % 4
                wy = list(self.tk_rysp_all)
                if ryr_rel[sl4] is not None:
                    wy.append(ryr_rel[sl4])
                ldr0 = st_ryr.dma(P, "sy", ryr[:, sl4, 0:512],
                                  d["routedy_sp"][tb * 2 + mt][:, 0:512],
                                  waits=wy)
                ldr1 = st_ryr.dma(P, "sy", ryr[:, sl4, 512:1024],
                                  d["routedy_sp"][tb * 2 + mt][:, 512:1024],
                                  waits=wy)
                wv2 = [(pe, vd), ldr0, ldr1]
                if y2p_rel[sl2] is not None:
                    wv2 += y2p_rel[sl2]
                P.op("ve", lambda e, sl2=sl2, sl4=sl4, sld=sld: e.tensor_tensor(
                    out=y2p[:, sl2, :], in0=ps_d[sld][:], in1=ryr[:, sl4, :],
                    op=OP.add), waits=wv2, inc=ve)
                ryr_rel[sl4] = (ve, ve.n)
                psd_rel[sld] = (ve, ve.n)
                vy2 = ve.n
                tk = st_y2sp.dma(P, "sy", d["y2pre_sp"][tb * 2 + mt],
                                 y2p[:, sl2, :], waits=[(ve, vy2)])
                P.drain("ve")
                P.op("ve", lambda e, sl2=sl2: e.bn_stats(
                    out=bnst[:, sl2, 0:6], in_=y2p[:, sl2, 0:512]))
                P.op("ve", lambda e, sl2=sl2: e.bn_stats(
                    out=bnst[:, sl2, 6:12], in_=y2p[:, sl2, 512:1024]))
                P.drain("ve")
                P.op("ve", lambda e, sl2=sl2: e.bn_aggr(
                    out=mv[:, sl2, :], in_=bnst[:, sl2, :]), inc=ve)
                vmv = ve.n
                P.op("se", lambda e, sl2=sl2: e.activation(
                    out=lnv[:, sl2:sl2 + 1], in_=mv[:, sl2, 1:2],
                    func=AF.Ln, bias=self.epsv[:, 0:1]),
                    waits=[(ve, vmv), self.tk_eps])
                P.drain("se")
                P.op("se", lambda e, sl2=sl2: e.activation(
                    out=rstd[:, sl2:sl2 + 1], in_=lnv[:, sl2:sl2 + 1],
                    func=AF.Exp, scale=-0.5), inc=se)
                vrs = se.n
                P.op("ve", lambda e, sl2=sl2: e.tensor_scalar(
                    out=nmr[:, sl2:sl2 + 1], in0=mv[:, sl2, 0:1],
                    scalar1=rstd[:, sl2:sl2 + 1], scalar2=-1.0,
                    op0=OP.mult, op1=OP.mult), waits=[(se, vrs)], inc=ve)
                vnm = ve.n
                wo = [(ve, vnm)]
                if ob_rel[sl2] is not None:
                    wo.append(ob_rel[sl2])
                P.op("se", lambda e, sl2=sl2: e.activation(
                    out=ob[:, sl2, :], in_=y2p[:, sl2, :], func=AF.Identity,
                    scale=rstd[:, sl2:sl2 + 1], bias=nmr[:, sl2:sl2 + 1]),
                    waits=wo, inc=se)
                vob = se.n
                y2p_rel[sl2] = [tk, (se, vob)]
                tko = st_out.dma(P, "sy",
                                 d["out"][(tb * 2 + mt) * 128:
                                          (tb * 2 + mt + 1) * 128, :],
                                 ob[:, sl2, :], waits=[(se, vob)])
                ob_rel[sl2] = tko

        self.tk_y2sp_all = [(s, s.n) for s in st_y2sp.sems]
        P.wait("sy", *(self.tk_y2sp_all + [(s, s.n) for s in st_out.sems]))
        P.emit(nc, "p3")
        c4.close()

    def phase35(self):
        nc, d, sm = self.nc, self.d, self.sem
        ve, se = (sm[k] for k in ("ve", "se"))
        c5 = ExitStack()
        S = lambda n, s, t: c5.enter_context(nc.sbuf_tensor(n, s, t))
        y2r = S("y2r", [128, 2, 1024], F32)
        bnst = S("bnst5", [128, 12], F32)
        mv = S("mv5", [128, 2], F32)
        lnv = S("lnv5", [128, 1], F32)
        rstd = S("rstd5", [128, 1], F32)
        nmr = S("nmr5", [128, 1], F32)
        ob = S("ob", [128, 2, 1024], F32)
        P = Plan()
        st_y2r = self.new_stream("y2r", 2)
        st_out = self.new_stream("outs", 2)
        y2r_rel = [None, None]
        ob_rel = [None, None]
        for t in range(16):
            sl = t % 2
            w = list(self.tk_y2sp_all)
            if y2r_rel[sl] is not None:
                w.append(y2r_rel[sl])
            ldt = st_y2r.dma(P, "sy", y2r[:, sl, :], d["y2pre_sp"][t], waits=w)
            P.op("ve", lambda e, sl=sl: e.bn_stats(out=bnst[:, 0:6],
                                                   in_=y2r[:, sl, 0:512]),
                 waits=[ldt])
            P.op("ve", lambda e, sl=sl: e.bn_stats(out=bnst[:, 6:12],
                                                   in_=y2r[:, sl, 512:1024]))
            P.drain("ve")
            P.op("ve", lambda e: e.bn_aggr(out=mv[:], in_=bnst[:]), inc=ve)
            vmv = ve.n
            P.op("se", lambda e: e.activation(out=lnv[:], in_=mv[:, 1:2],
                                              func=AF.Ln, bias=self.epsv[:, 0:1]),
                 waits=[(ve, vmv), self.tk_eps])
            P.drain("se")
            P.op("se", lambda e: e.activation(out=rstd[:], in_=lnv[:],
                                              func=AF.Exp, scale=-0.5), inc=se)
            vrs = se.n
            P.op("ve", lambda e: e.tensor_scalar(
                out=nmr[:], in0=mv[:, 0:1], scalar1=rstd[:, 0:1], scalar2=-1.0,
                op0=OP.mult, op1=OP.mult), waits=[(se, vrs)], inc=ve)
            vnm = ve.n
            wo = [(ve, vnm)]
            if ob_rel[sl] is not None:
                wo.append(ob_rel[sl])
            P.op("se", lambda e, sl=sl: e.activation(
                out=ob[:, sl, :], in_=y2r[:, sl, :], func=AF.Identity,
                scale=rstd[:, 0:1], bias=nmr[:, 0:1]), waits=wo, inc=se)
            y2r_rel[sl] = (se, se.n)
            vob = se.n
            tk = st_out.dma(P, "sy", d["out"][t * 128:(t + 1) * 128, :],
                            ob[:, sl, :], waits=[(se, vob)])
            ob_rel[sl] = tk
        P.wait("sy", *[(s, s.n) for s in st_out.sems])
        P.emit(nc, "p35")
        c5.close()

    def build(self, upto=99):
        self.declare()
        with self.ctx:
            self.phase0()
            if upto >= 1:
                self.phase1()
            if upto >= 2:
                self.phase15()
            if upto >= 3:
                self.phase2()
            if upto >= 4:
                self.phase3()
            for stk in (getattr(self, "ce", None), getattr(self, "ca", None)):
                if stk is not None:
                    stk.close()
            self.ce = self.ca = None
        return self.nc


def _host_consts():
    NEG = np.float32(NEGF)
    jj = np.arange(128)
    qq = np.arange(128)[:, None]
    maskA = np.where(jj[None, :] >= qq, 0.0, NEG).astype(np.float32)
    maskB = np.where(jj[None, :] < qq, 0.0, NEG).astype(np.float32)
    maskAB = np.concatenate([maskA, maskB], axis=1)
    ident = np.eye(128, dtype=np.float32)
    zpool = np.zeros((128, 129), np.float32)
    zpool[:, 64] = 1.0 / 256.0
    rep2 = np.zeros((2, 128), np.float32)
    rep2[0, 0:64] = 0.5
    rep2[1, 64:128] = 0.5
    ue16 = np.zeros((16, 2048), np.float32)
    for e in range(16):
        ue16[e, e * 128:(e + 1) * 128] = 1.0
    return maskAB, maskA, maskB, ident, zpool, rep2, ue16


def make_in_maps(inputs):
    f32c = lambda a: np.ascontiguousarray(np.asarray(a, np.float32))
    x = f32c(inputs["L0_attn_x"]).reshape(16384, 1024)
    qkv_b = f32c(inputs["L0_attn_qkv_b"])
    maskAB, maskA, maskB, ident, zpool, rep2, ue16 = _host_consts()
    glgr = np.zeros((128, 4), np.float32)
    gl = f32c(inputs["L0_attn_gl_w"])[:, 0]
    gr = f32c(inputs["L0_attn_gr_w"])[:, 0]
    glgr[0:64, 0] = gl
    glgr[64:128, 1] = gl
    glgr[0:64, 2] = gr
    glgr[64:128, 3] = gr
    base = {
        "x_all": x,
        "qkv_w": f32c(inputs["L0_attn_qkv_w"]),
        "rq_w": f32c(inputs["L0_attn_rq_w"]),
        "rkv_w": f32c(inputs["L0_attn_rkv_w"]),
        "rout_w": f32c(inputs["L0_attn_rout_w"]),
        "out_w": f32c(inputs["L0_attn_out_w"]),
        "router_wT": np.ascontiguousarray(f32c(inputs["L0_ffn_router_w"]).T),
        "gate_w": f32c(inputs["L0_ffn_gate_w"]),
        "up_w": f32c(inputs["L0_ffn_up_w"]),
        "down_w": f32c(inputs["L0_ffn_down_w"]),
        "sgw_t": np.ascontiguousarray(f32c(inputs["L0_ffn_shared_gate_w"]).T),
        "suw_t": np.ascontiguousarray(f32c(inputs["L0_ffn_shared_up_w"]).T),
        "sdw_t": np.ascontiguousarray(f32c(inputs["L0_ffn_shared_down_w"]).T),
        "ident_in": ident,
        "zpool_in": zpool,
        "rep2_in": rep2,
        "ue16_in": ue16,
        "glgr_in": glgr,
        "maskAB_in": maskAB,
        "qkvbt_in": np.ascontiguousarray(qkv_b.reshape(24, 128).T),
        "rqbt_in": np.ascontiguousarray(
            f32c(inputs["L0_attn_rq_b"]).reshape(2, 128).T),
        "rkvbk_in": np.ascontiguousarray(
            f32c(inputs["L0_attn_rkv_b"])[:256].reshape(2, 128).T),
        "rvb_in": np.ascontiguousarray(
            np.tile(f32c(inputs["L0_attn_rkv_b"])[256:][None, :], (64, 1))),
        "routbt_in": np.ascontiguousarray(
            f32c(inputs["L0_attn_rout_b"]).reshape(8, 128).T),
        "outbrep_in": np.ascontiguousarray(
            np.tile(f32c(inputs["L0_attn_out_b"])[None, :], (128, 1))),
        "vbrep_in": np.ascontiguousarray(np.tile(qkv_b[2048:][None, :], (128, 1))),
        "corr_in": np.ascontiguousarray(
            np.tile(f32c(inputs["L0_ffn_correction_bias"])[None, :], (128, 1))),
    }
    NEG = np.float32(NEGF)
    in_maps = []
    for c in range(8):
        t0 = c * 2048
        m = dict(base)
        m["x_own"] = np.ascontiguousarray(x[t0:t0 + 2048])
        m["x_halo"] = (np.zeros((128, 1024), np.float32) if c == 0
                       else np.ascontiguousarray(x[t0 - 128:t0]))
        mA0 = np.full((128, 128), NEG, np.float32) if c == 0 else maskA
        m["maskA0B_in"] = np.ascontiguousarray(np.concatenate([mA0, maskB], 1))
        lat = np.zeros((16, 64), np.float32)
        for qt in range(16):
            cmax = (t0 + qt * 128) // 256
            lat[qt] = np.where(np.arange(64) < cmax, 0.0, NEG)
        m["latmask_in"] = np.ascontiguousarray(
            np.broadcast_to(lat[None, :, :], (128, 16, 64)).copy())
        in_maps.append(m)
    return in_maps


def kernel(**inputs):
    in_maps = make_in_maps(inputs)
    nc = Builder().build()
    res = run_bass_kernel_spmd(nc, in_maps, core_ids=list(range(8)))
    out = np.concatenate([r["out"] for r in res.results], axis=0)
    return out.reshape(1, 16384, 1024).astype(np.float32)


# version 15
# speedup vs baseline: 1.0070x; 1.0070x over previous
import numpy as np
import concourse.bass as bass
import concourse.mybir as mybir
from concourse.alu_op_type import AluOpType as OP
from concourse.bass_utils import run_bass_kernel_spmd
from contextlib import ExitStack

F32 = mybir.dt.float32
BF16 = mybir.dt.bfloat16
AF = mybir.ActivationFunctionType
AXX = mybir.AxisListType.X

NCH = 4
NEGF = -1e9


class Sem:
    def __init__(self, h, name):
        self.h, self.name, self.n = h, name, 0


class Rot:
    def __init__(self, n):
        self.n, self.rel, self.i = n, [None] * n, 0

    def acquire(self):
        s = self.i % self.n
        self.i += 1
        w = self.rel[s]
        return s, ([w] if w is not None else [])

    def release(self, slot, sem, val):
        self.rel[slot] = (sem, val)


class Stream:

    def __init__(self, sems):
        self.sems = sems
        self.i = 0

    def dma(self, P, eng, dst, src, waits=()):
        s = self.sems[self.i % len(self.sems)]
        self.i += 1
        P.op(eng, lambda e, d=dst, s_=src: e.dma_start(out=d, in_=s_),
             waits=waits, inc=s, incn=16)
        return (s, s.n)


class Plan:
    ENGS = ("te", "ve", "se", "sy", "ge")

    def __init__(self):
        self.ops = {k: [] for k in self.ENGS}
        self.last_wait = {}

    def op(self, eng, fn, waits=(), inc=None, incn=1):
        w = []
        for (s, v) in waits:
            if v > self.last_wait.get((eng, s.name), -1):
                self.last_wait[(eng, s.name)] = v
                w.append((s, v))
        if inc is not None:
            inc.n += incn

        def run(e, fn=fn, w=tuple(w), inc=inc, incn=incn):
            for (s, v) in w:
                e.wait_ge(s.h, v)
            ins = fn(e)
            if inc is not None:
                ins.then_inc(inc.h, incn)
        self.ops[eng].append(run)

    def drain(self, eng):
        self.ops[eng].append(lambda e: e.drain())

    def wait(self, eng, *waits):
        self.op(eng, lambda e: None, waits=waits)

    def emit(self, nc, name):
        with nc.Block(name, no_gpsimd_drain=True) as blk:
            @blk.tensor
            def _(e):
                for f in self.ops["te"]:
                    f(e)

            @blk.vector
            def _(e):
                for f in self.ops["ve"]:
                    f(e)

            @blk.scalar
            def _(e):
                for f in self.ops["se"]:
                    f(e)

            @blk.sync
            def _(e):
                for f in self.ops["sy"]:
                    f(e)

            @blk.gpsimd
            def _(e):
                for f in self.ops["ge"]:
                    f(e)


def R3(a):
    return a.rearrange("(n p) m -> p n m", p=128)


class Builder:
    def __init__(self, p0_nch=NCH, p0_pool=True, p0_attn=True, p0_proj=True,
                 p0_st=6, p2_mode=4, p15_mode=3):
        self.p0_nch = p0_nch
        self.p0_pool = p0_pool
        self.p0_attn = p0_attn
        self.p0_proj = p0_proj
        self.p0_st = p0_st
        self.p2_mode = p2_mode
        self.p15_mode = p15_mode
        self.nc = bass.Bass("TRN2", target_bir_lowering=False)
        self.ctx = ExitStack()

    def declare(self):
        nc = self.nc
        DT = nc.dram_tensor
        g = self.d = {}
        for nm, shp in [("x_all", [16384, 1024]), ("x_own", [2048, 1024]),
                        ("x_halo", [128, 1024]), ("qkv_w", [1024, 3072]),
                        ("rq_w", [1024, 256]), ("rkv_w", [1024, 512]),
                        ("rout_w", [256, 1024]), ("out_w", [1024, 1024]),
                        ("router_wT", [1024, 16]), ("gate_w", [16, 1024, 256]),
                        ("up_w", [16, 1024, 256]), ("down_w", [16, 256, 1024]),
                        ("sgw_t", [1024, 2048]), ("suw_t", [1024, 2048]),
                        ("sdw_t", [2048, 1024]), ("ident_in", [128, 128]),
                        ("zpool_in", [128, 129]), ("rep2_in", [2, 128]),
                        ("ue16_in", [16, 2048]), ("glgr_in", [128, 4]),
                        ("maskAB_in", [128, 256]), ("maskA0B_in", [128, 256]),
                        ("latmask_in", [128, 16, 64]), ("qkvbt_in", [128, 24]),
                        ("rqbt_in", [128, 2]), ("rkvbk_in", [128, 2]),
                        ("rvb_in", [64, 256]), ("routbt_in", [128, 8]),
                        ("outbrep_in", [128, 1024]), ("vbrep_in", [128, 1024]),
                        ("corr_in", [128, 16])]:
            g[nm] = DT(nm, shp, F32, kind="ExternalInput").ap()
        dbg_kind = "ExternalOutput" if getattr(self, "dbg_spills", False) else "Internal"
        for nm in ("local_sp", "y0_sp"):
            pass
        for nm, shp, dt in [("local_sp", [128, 8, 2048], BF16),
                            ("y0_sp", [16, 128, 1024], F32),
                            ("y1_sp", [16, 128, 1024], F32),
                            ("y1fm_sp", [128, 8, 2048], BF16),
                            ("routedy_sp", [16, 128, 1024], F32),
                            ("y2pre_sp", [16, 128, 1024], F32),
                            ("dwbf_sp", [16, 128, 2, 1024], BF16),
                            ("uwbf_sp", [16, 128, 8, 256], BF16),
                            ("swg_sp", [128, 8, 2048], BF16),
                            ("swu_sp", [128, 8, 2048], BF16),
                            ("swd_sp", [128, 16, 1024], BF16)]:
            g[nm] = DT(nm, shp, dt,
                       kind=(dbg_kind if nm in ("local_sp", "y0_sp")
                             else "Internal")).ap()
        g["out"] = DT("out", [2048, 1024], F32, kind="ExternalOutput").ap()

        S = lambda n, s, t: self.ctx.enter_context(nc.sbuf_tensor(n, s, t))
        self.ident = S("ident", [128, 128], BF16)
        self.zpool = S("zpool", [128, 129], BF16)
        self.rep2 = S("rep2", [2, 128], BF16)
        self.ue16 = S("ue16", [16, 2048], BF16)
        self.glgr = S("glgr", [128, 4], BF16)
        self.maskAB = S("maskAB", [128, 256], F32)
        self.maskA0B = S("maskA0B", [128, 256], F32)
        self.latmask = S("latmask", [128, 16, 64], F32)
        self.qkvbt = S("qkvbt", [128, 24], F32)
        self.rqbt = S("rqbt", [128, 2], F32)
        self.rkvbk = S("rkvbk", [128, 2], F32)
        self.rvb = S("rvb", [64, 256], F32)
        self.routbt = S("routbt", [128, 8], F32)
        self.outbrep = S("outbrep", [128, 1024], F32)
        self.vbrep = S("vbrep", [128, 1024], F32)
        self.corr = S("corr", [128, 16], F32)
        self.qb8 = S("qb8", [128, 8], F32)
        self.epsv = S("epsv", [128, 1], F32)
        self.rqb8 = S("rqb8", [128, 2], F32)
        self.rq_sb = S("rq_sb", [128, 2, 2048], BF16)
        self.pooledfm = S("pooledfm", [128, 8, 64], BF16)
        self.rkfm = S("rkfm", [128, 2, 64], BF16)
        self.rvlm = S("rvlm", [64, 256], BF16)
        self.combT = S("combT", [16, 2048], BF16)

        self.sem = {}
        for nm in ("pe", "pp", "ve", "se"):
            self.sem[nm] = Sem(self.ctx.enter_context(nc.semaphore(nm)), nm)

    def new_sem(self, nm):
        return Sem(self.ctx.enter_context(self.nc.semaphore(nm)), nm)

    def new_stream(self, nm, n):
        return Stream([self.new_sem(f"{nm}{i}") for i in range(n)])

    def phase0(self):
        nc, d, sm = self.nc, self.d, self.sem
        pe, pp, ve, se = (sm[k] for k in ("pe", "pp", "ve", "se"))
        c0 = ExitStack()
        S = lambda n, s, t: c0.enter_context(nc.sbuf_tensor(n, s, t))
        PS = lambda n, s, t: c0.enter_context(nc.psum_tensor(n, s, t))
        self.ca = ExitStack()
        SA = lambda n, s, t: self.ca.enter_context(nc.sbuf_tensor(n, s, t))
        rkvw = SA("rkvw", [128, 8, 512], BF16)
        self.routw = SA("routw", [128, 2, 1024], BF16)
        self.outw = SA("outw", [128, 8, 1024], BF16)
        self.routerw = SA("routerw", [128, 8, 16], BF16)

        qkvw = S("qkvw", [128, 8, 3072], BF16)
        rqw = S("rqw", [128, 8, 256], BF16)
        xown = S("xown", [128, 4, 1024], BF16)
        xhalo = S("xhalo", [128, 1024], BF16)
        xfm = S("xfm", [128, 8, 512], BF16)
        xfmh = S("xfmh", [128, 8, 128], BF16)
        q_sb = S("q_sb", [128, 8, 512], BF16)
        k_sb = S("k_sb", [128, 8, 512], BF16)
        ktail = S("ktail", [128, 8, 128], BF16)
        v_sb = S("v_sb", [128, 4, 1024], BF16)
        vtail = S("vtail", [128, 1024], BF16)
        local_sb = S("local_sb", [128, 8, 512], BF16)
        RN = 10
        ring = S("ring", [128, RN, 1024], BF16)
        maskedb = S("maskedb", [128, 2, 512], F32)
        p_b = S("p_b", [128, 2, 512], BF16)
        pn_b = S("pn_b", [128, 2, 512], BF16)
        pT_b = S("pT_b", [128, 2, 512], BF16)
        macc = S("macc", [128, 16], F32)
        mn_sb = S("mn_sb", [128, 16], F32)
        ssum = S("ssum", [128, 16], F32)
        rvv = S("rvv", [128, 16], F32)
        pooledlm = S("pooledlm", [64, 1024], BF16)

        ps_pool = PS("ps_pool", [64, 1024], F32)
        ps_pr = [PS(f"ps_pr{i}", [128, 512], F32) for i in range(2)]
        ps_sc = [PS(f"ps_sc{i}", [128, 512], F32) for i in range(2)]
        ps_bt = [PS(f"ps_bt{i}", [128, 512], BF16) for i in range(2)]

        P = Plan()
        st_gec = self.new_stream("gec", 1)
        st_gec.dma(P, "ge", self.ident[:], d["ident_in"])
        st_gec.dma(P, "ge", self.zpool[:], d["zpool_in"])
        st_gec.dma(P, "ge", self.glgr[:], d["glgr_in"])
        st_gec.dma(P, "ge", self.rep2[:], d["rep2_in"])
        tk_gec = st_gec.dma(P, "ge", self.ue16[:], d["ue16_in"])
        self.tk_gec = tk_gec
        s_eps = self.new_sem("eps")
        P.op("ge", lambda e: e.memset(self.epsv[:], 1e-5), inc=s_eps)
        self.tk_eps = (s_eps, 1)
        s_qkvw = self.new_sem("sqkvw")
        st1 = Stream([s_qkvw])
        tk_qkvw = st1.dma(P, "ge", qkvw[:], R3(d["qkv_w"]))
        tk_rqw = Stream([self.new_sem("srqw")]).dma(P, "ge", rqw[:], R3(d["rq_w"]))
        s_halo = Stream([self.new_sem("shalo")])
        tk_halo = s_halo.dma(P, "ge", xhalo[:], d["x_halo"])
        st_xo = self.new_stream("xo", 2)
        ld_own = {0: st_xo.dma(P, "ge", xown[:], R3(d["x_own"][0:512, :]))}
        self.tk_rkvw = Stream([self.new_sem("srkvw")]).dma(P, "ge", rkvw[:],
                                                           R3(d["rkv_w"]))
        self.tk_routw = Stream([self.new_sem("sroutw")]).dma(
            P, "ge", self.routw[:], R3(d["rout_w"]))
        self.tk_outw = Stream([self.new_sem("soutw")]).dma(
            P, "ge", self.outw[:], R3(d["out_w"]))
        self.tk_routerw = Stream([self.new_sem("srouterw")]).dma(
            P, "ge", self.routerw[:], R3(d["router_wT"]))
        st_syc = self.new_stream("syc", 1)
        for dst, src in [(self.maskAB[:], d["maskAB_in"]),
                         (self.maskA0B[:], d["maskA0B_in"]),
                         (self.latmask[:], d["latmask_in"]),
                         (self.qkvbt[:], d["qkvbt_in"]),
                         (self.rqbt[:], d["rqbt_in"]),
                         (self.rkvbk[:], d["rkvbk_in"]),
                         (self.rvb[:], d["rvb_in"]),
                         (self.routbt[:], d["routbt_in"]),
                         (self.outbrep[:], d["outbrep_in"]),
                         (self.vbrep[:], d["vbrep_in"]),
                         (self.corr[:], d["corr_in"])]:
            self.tk_consts = st_syc.dma(P, "sy", dst, src)

        P.op("se", lambda e: e.activation(out=self.qb8[:], in_=self.qkvbt[:, 0:8],
                                          func=AF.Identity, scale=0.125),
             waits=[self.tk_consts], inc=se)
        P.op("se", lambda e: e.activation(out=self.rqb8[:], in_=self.rqbt[:],
                                          func=AF.Identity, scale=0.125), inc=se)
        se_bias = se.n

        ring_ldx = {}
        ring_emitted = [0]
        pool_emitted = [0]
        st_xr = self.new_stream("xr", 5)

        def emit_ring(upto_pair):
            while ring_emitted[0] < min(upto_pair, 64):
                j = ring_emitted[0]
                w = []
                need = 2 * j + 2 - RN
                if need > 0:
                    w.append((pp, need))
                ring_ldx[j] = st_xr.dma(
                    P, "ge", ring[:, (2 * j) % RN:(2 * j) % RN + 2, :],
                    R3(d["x_all"][2 * j * 128:(2 * j + 2) * 128, :]), waits=w)
                ring_emitted[0] += 1

        def emit_pool(n):
            if not self.p0_pool:
                return
            for _ in range(n):
                t = pool_emitted[0]
                if t >= 128:
                    return
                emit_ring(t // 2 + RN // 2)
                w = [ring_ldx[t // 2], tk_gec]
                lh = self.zpool[:, 64 - t // 2:128 - t // 2]
                for hf in range(2):
                    P.op("te", lambda e, lh=lh, t=t, hf=hf: e.matmul(
                        ps_pool[0:64, hf * 512:(hf + 1) * 512], lh,
                        ring[:, t % RN, hf * 512:(hf + 1) * 512],
                        start=(t == 0), stop=(t == 127)),
                        waits=w if hf == 0 else (),
                        inc=pp if hf == 1 else None)
                    w = ()
                pool_emitted[0] += 1

        rot_pr, rot_bt = Rot(2), Rot(2)

        def transpose_group(srcs, dsts, wait_extra=()):
            slot, w = rot_bt.acquire()
            w = list(w) + list(wait_extra) + [tk_gec]
            for i, a in enumerate(srcs):
                P.op("te", lambda e, sl=slot, i=i, a=a: e.transpose(
                    ps_bt[sl][:, i * 128:(i + 1) * 128], a, self.ident[:]),
                    waits=w if i == 0 else (),
                    inc=pe if i == len(srcs) - 1 else None)
                w = ()
            mmv = pe.n
            for i, dd in enumerate(dsts):
                P.op("se", lambda e, sl=slot, i=i, dd=dd: e.copy(
                    out=dd, in_=ps_bt[sl][:, i * 128:(i + 1) * 128]),
                    waits=[(pe, mmv)], inc=se if i == len(dsts) - 1 else None)
            rot_bt.release(slot, se, se.n)

        def proj(lhs_fn, rhs_fn, nk, evac, wait_first=(), ncols=512, nrows=128):
            slot, w = rot_pr.acquire()
            w = list(w) + list(wait_first)
            for kk in range(nk):
                P.op("te", lambda e, sl=slot, kk=kk, nc2=ncols, nr=nrows: e.matmul(
                    ps_pr[sl][0:nr, 0:nc2], lhs_fn(kk), rhs_fn(kk),
                    start=(kk == 0), stop=(kk == nk - 1)),
                    waits=w if kk == 0 else (),
                    inc=pe if kk == nk - 1 else None)
                w = ()
            evac(slot, pe.n)

        rel_masked = [None, None]
        rel_p = [None, None]
        rel_pn = [None, None]
        rel_pT = [None, None]
        locsb_rel = [None]
        rel_sc = [None, None]
        st_locsp = self.new_stream("locsp", 1)

        for c in range(self.p0_nch):
            halo_se = None
            if c == 0:
                for fc in range(8):
                    transpose_group([xhalo[:, fc * 128:(fc + 1) * 128]],
                                    [xfmh[:, fc, :]],
                                    wait_extra=[tk_halo])
                halo_se = se.n
            for fc in range(8):
                transpose_group(
                    [xown[:, tt, fc * 128:(fc + 1) * 128] for tt in range(4)],
                    [xfm[:, fc, tt * 128:(tt + 1) * 128] for tt in range(4)],
                    wait_extra=[ld_own[c]])
            tr_se = se.n
            tr_pe = pe.n
            if c + 1 < NCH:
                ld_own[c + 1] = st_xo.dma(
                    P, "ge", xown[:], R3(d["x_own"][(c + 1) * 512:(c + 2) * 512, :]),
                    waits=[(pe, tr_pe)])

            for of in (range(8) if self.p0_proj else []):
                def ev(slot, mmv, of=of):
                    P.op("se", lambda e, sl=slot, of=of: e.activation(
                        out=q_sb[:, of, :], in_=ps_pr[sl][:], func=AF.Identity,
                        scale=0.125, bias=self.qb8[:, of:of + 1]),
                        waits=[(pe, mmv), (se, se_bias)], inc=se)
                    rot_pr.release(slot, se, se.n)
                proj(lambda kk, of=of: qkvw[:, kk, of * 128:(of + 1) * 128],
                     lambda kk: xfm[:, kk, :], 8, ev,
                     wait_first=[(se, tr_se), tk_qkvw])

            if c == 0 and self.p0_proj:
                for of in range(8):
                    def ev(slot, mmv, of=of):
                        P.op("se", lambda e, sl=slot, of=of: e.activation(
                            out=ktail[:, of, :], in_=ps_pr[sl][:, 0:128],
                            func=AF.Identity, bias=self.qkvbt[:, 8 + of:9 + of]),
                            waits=[(pe, mmv)], inc=se)
                        rot_pr.release(slot, se, se.n)
                    proj(lambda kk, of=of: qkvw[:, kk, 1024 + of * 128:1152 + of * 128],
                         lambda kk: xfmh[:, kk, :], 8, ev,
                         wait_first=[(se, halo_se)], ncols=128)
                for nh in range(2):
                    def ev(slot, mmv, nh=nh):
                        P.op("ve", lambda e, sl=slot, nh=nh: e.tensor_tensor(
                            out=vtail[:, nh * 512:(nh + 1) * 512],
                            in0=ps_pr[sl][:],
                            in1=self.vbrep[:, nh * 512:(nh + 1) * 512], op=OP.add),
                            waits=[(pe, mmv), self.tk_consts], inc=ve)
                        rot_pr.release(slot, ve, ve.n)
                    proj(lambda kk: xfmh[:, kk, :],
                         lambda kk, nh=nh: qkvw[:, kk, 2048 + nh * 512:2560 + nh * 512],
                         8, ev)
            elif self.p0_proj:
                P.op("ve", lambda e: e.tensor_copy(out=ktail[:],
                                                   in_=k_sb[:, :, 384:512]),
                     waits=[(pe, pe.n)], inc=ve)
                P.op("ve", lambda e: e.tensor_copy(out=vtail[:], in_=v_sb[:, 3, :]),
                     inc=ve)
            tail_ve = ve.n

            for of in (range(8) if self.p0_proj else []):
                def ev(slot, mmv, of=of):
                    P.op("se", lambda e, sl=slot, of=of: e.activation(
                        out=k_sb[:, of, :], in_=ps_pr[sl][:], func=AF.Identity,
                        bias=self.qkvbt[:, 8 + of:9 + of]),
                        waits=[(pe, mmv), (ve, tail_ve)], inc=se)
                    rot_pr.release(slot, se, se.n)
                proj(lambda kk, of=of: qkvw[:, kk, 1024 + of * 128:1152 + of * 128],
                     lambda kk: xfm[:, kk, :], 8, ev)
            k_se = se.n

            for mt in (range(4) if self.p0_proj else []):
                for nh in range(2):
                    def ev(slot, mmv, mt=mt, nh=nh):
                        P.op("ve", lambda e, sl=slot, mt=mt, nh=nh: e.tensor_tensor(
                            out=v_sb[:, mt, nh * 512:(nh + 1) * 512],
                            in0=ps_pr[sl][:],
                            in1=self.vbrep[:, nh * 512:(nh + 1) * 512], op=OP.add),
                            waits=[(pe, mmv), (ve, tail_ve), self.tk_consts],
                            inc=ve)
                        rot_pr.release(slot, ve, ve.n)
                    proj(lambda kk, mt=mt: xfm[:, kk, mt * 128:(mt + 1) * 128],
                         lambda kk, nh=nh: qkvw[:, kk, 2048 + nh * 512:2560 + nh * 512],
                         8, ev)
            v_ve = ve.n

            for of2 in (range(2) if self.p0_proj else []):
                def ev(slot, mmv, of2=of2, c=c):
                    P.op("se", lambda e, sl=slot, of2=of2, c=c: e.activation(
                        out=self.rq_sb[:, of2, c * 512:(c + 1) * 512],
                        in_=ps_pr[sl][:], func=AF.Identity, scale=0.125,
                        bias=self.rqb8[:, of2:of2 + 1]),
                        waits=[(pe, mmv)], inc=se)
                    rot_pr.release(slot, se, se.n)
                proj(lambda kk, of2=of2: rqw[:, kk, of2 * 128:(of2 + 1) * 128],
                     lambda kk: xfm[:, kk, :], 8, ev,
                     wait_first=[tk_rqw])

            emit_pool(8)

            G = 32
            smm = [0] * G
            ttrv = [0] * G
            expv = [0] * G
            nrm = [0] * G
            evT = [0] * G

            def stA(g, c=c):
                qt, f = divmod(g, 8)
                qs = qt * 128
                pr = g % 2
                w = [(se, k_se)]
                if rel_sc[0] is not None:
                    w.append(rel_sc[0])
                for hh in range(2):
                    off = 64 * hh
                    lhsT = q_sb[off:off + 64, f, qs:qs + 128]
                    if qt == 0:
                        mms = [(lhsT, ktail[off:off + 64, f, :], 0, 128),
                               (lhsT, k_sb[off:off + 64, f, 0:128], 128, 128)]
                    else:
                        mms = [(lhsT,
                                k_sb[off:off + 64, f, qs - 128:qs + 128],
                                0, 256)]
                    for i, (lh, rh, cc0, nn) in enumerate(mms):
                        last = (hh == 1 and i == len(mms) - 1)
                        P.op("te", lambda e, hh=hh, lh=lh, rh=rh, cc0=cc0,
                             nn=nn, pr=pr: e.matmul(
                                 ps_sc[hh][:, pr * 256 + cc0:pr * 256 + cc0 + nn],
                                 lh, rh, start=True, stop=True),
                             waits=w if (hh == 0 and i == 0) else (),
                             inc=pe if last else None)
                        w = ()
                smm[g] = pe.n

            def stB(g, c=c):
                qt, f = divmod(g, 8)
                mb = g % 2
                m = self.maskA0B[:] if (c == 0 and qt == 0) else self.maskAB[:]
                w = [(pe, smm[g]), self.tk_consts]
                if rel_masked[mb] is not None:
                    w.append(rel_masked[mb])
                for hh in range(2):
                    P.op("ve", lambda e, mb=mb, hh=hh, m=m:
                         e.tensor_tensor(
                             out=maskedb[:, mb, hh * 256:(hh + 1) * 256],
                             in0=ps_sc[hh][:, mb * 256:mb * 256 + 256],
                             in1=m, op=OP.add),
                         waits=w if hh == 0 else ())
                    w = ()
                P.drain("ve")
                mb3 = maskedb[:, mb, :].rearrange("p (s x) -> p s x", s=2)
                P.op("ve", lambda e, f=f, mb3=mb3: e.tensor_reduce(
                    out=mn_sb[:, 2 * f:2 * f + 2], in_=mb3, axis=AXX,
                    op=OP.max, negate=True))
                P.drain("ve")
                P.op("ve", lambda e, f=f: e.tensor_scalar(
                    out=mn_sb[:, 2 * f:2 * f + 2],
                    in0=mn_sb[:, 2 * f:2 * f + 2],
                    scalar1=0.0, scalar2=None, op0=OP.min, op1=OP.bypass),
                    inc=ve)
                ttrv[g] = ve.n
                rel_sc[0] = (ve, ve.n)

            def stC(g, c=c):
                qt, f = divmod(g, 8)
                mb = g % 2
                w = [(ve, ttrv[g])]
                if rel_p[mb] is not None:
                    w.append(rel_p[mb])
                for hh in range(2):
                    P.op("se", lambda e, mb=mb, hh=hh, f=f: e.activation(
                        out=p_b[:, mb, hh * 256:(hh + 1) * 256],
                        in_=maskedb[:, mb, hh * 256:(hh + 1) * 256],
                        func=AF.Exp,
                        bias=mn_sb[:, 2 * f + hh:2 * f + hh + 1],
                        accum_out=ssum[:, 2 * f + hh:2 * f + hh + 1]),
                        waits=w if hh == 0 else (),
                        inc=se if hh == 1 else None)
                    w = ()
                expv[g] = se.n
                rel_masked[mb] = (se, se.n)

            def stD(g, c=c):
                qt, f = divmod(g, 8)
                mb = g % 2
                P.op("ve", lambda e, f=f: e.tensor_scalar(
                    out=rvv[:, 2 * f:2 * f + 2], in0=ssum[:, 2 * f:2 * f + 2],
                    scalar1=1e-20, scalar2=None, op0=OP.add, op1=OP.bypass),
                    waits=[(se, expv[g])])
                P.drain("ve")
                P.op("ve", lambda e, f=f: e.reciprocal(
                    out=rvv[:, 2 * f:2 * f + 2], in_=rvv[:, 2 * f:2 * f + 2]))
                P.drain("ve")
                w = []
                if rel_pn[mb] is not None:
                    w.append(rel_pn[mb])
                for hh in range(2):
                    P.op("ve", lambda e, mb=mb, hh=hh, f=f: e.tensor_scalar(
                        out=pn_b[:, mb, hh * 256:(hh + 1) * 256],
                        in0=p_b[:, mb, hh * 256:(hh + 1) * 256],
                        scalar1=rvv[:, 2 * f + hh:2 * f + hh + 1],
                        scalar2=None, op0=OP.mult, op1=OP.bypass),
                        waits=w if hh == 0 else (),
                        inc=ve if hh == 1 else None)
                    w = ()
                nrm[g] = ve.n
                rel_p[mb] = (ve, ve.n)

            def stE(g, c=c):
                mb = g % 2
                slot, w = rot_bt.acquire()
                w = list(w) + [(ve, nrm[g])]
                for i in range(4):
                    P.op("te", lambda e, sl=slot, mb=mb, i=i: e.transpose(
                        ps_bt[sl][:, i * 128:(i + 1) * 128],
                        pn_b[:, mb, i * 128:(i + 1) * 128], self.ident[:]),
                        waits=w if i == 0 else (),
                        inc=pe if i == 3 else None)
                    w = ()
                rel_pn[mb] = (pe, pe.n)
                wv = [(pe, pe.n)]
                if rel_pT[mb] is not None:
                    wv.append(rel_pT[mb])
                P.op("se", lambda e, sl=slot, mb=mb: e.copy(
                    out=pT_b[:, mb, :], in_=ps_bt[sl][:]), waits=wv, inc=se)
                evT[g] = se.n
                rot_bt.release(slot, se, se.n)

            def stG(g, c=c):
                qt, f = divmod(g, 8)
                qs = qt * 128
                mb = g % 2
                slot, w = rot_pr.acquire()
                w = list(w) + [(se, evT[g]), (ve, v_ve)]
                for hh in range(2):
                    h = 2 * f + hh
                    for kt in range(2):
                        if qt == 0 and kt == 0:
                            vap = vtail[:, h * 64:(h + 1) * 64]
                        else:
                            vap = v_sb[:, qt - 1 + kt, h * 64:(h + 1) * 64]
                        P.op("te",
                             lambda e, sl=slot, vap=vap, mb=mb, hh=hh, kt=kt:
                             e.matmul(ps_pr[sl][64 * hh:64 * hh + 64, 0:128],
                                      vap,
                                      pT_b[:, mb, (2 * hh + kt) * 128:
                                           (2 * hh + kt + 1) * 128],
                                      start=(kt == 0), stop=(kt == 1)),
                             waits=w, inc=pe if (hh * 2 + kt) == 3 else None)
                        w = ()
                rel_pT[mb] = (pe, pe.n)
                wev = [(pe, pe.n)]
                if locsb_rel[0] is not None:
                    wev.append(locsb_rel[0])
                P.op("se", lambda e, sl=slot, f=f, qs=qs: e.copy(
                    out=local_sb[:, f, qs:qs + 128], in_=ps_pr[sl][:, 0:128]),
                    waits=wev, inc=se)
                rot_pr.release(slot, se, se.n)

            ST = self.p0_st
            if self.p0_attn:
                for i in range(G + 5):
                    if 1 <= i <= G and ST >= 2:
                        stB(i - 1)
                    if i < G:
                        stA(i)
                    if 2 <= i <= G + 1 and ST >= 3:
                        stC(i - 2)
                    if 3 <= i <= G + 2 and ST >= 4:
                        stD(i - 3)
                    if 4 <= i <= G + 3 and ST >= 5:
                        stE(i - 4)
                    if 5 <= i <= G + 4 and ST >= 6:
                        stG(i - 5)
                    if i < G and i % 8 == 7:
                        emit_pool(6)

            self.tk_locsp = st_locsp.dma(
                P, "sy", d["local_sp"][:, :, c * 512:(c + 1) * 512], local_sb[:],
                waits=[(se, se.n)])
            locsb_rel[0] = self.tk_locsp

        emit_pool(128)

        if not self.p0_pool:
            P.wait("sy", self.tk_locsp)
            P.emit(nc, "p0")
            c0.close()
            return
        P.op("ve", lambda e: e.tensor_copy(out=pooledlm[:], in_=ps_pool[0:64, :]),
             waits=[(pp, 128)], inc=ve)
        plm = ve.n
        for g2 in range(2):
            slot, w = rot_bt.acquire()
            w = list(w) + [(ve, plm)]
            for i in range(4):
                fc = g2 * 4 + i
                P.op("te", lambda e, sl=slot, i=i, fc=fc: e.transpose(
                    ps_bt[sl][:, i * 128:i * 128 + 64],
                    pooledlm[0:64, fc * 128:(fc + 1) * 128],
                    self.ident[0:64, 0:64]),
                    waits=w if i == 0 else (), inc=pe if i == 3 else None)
                w = ()
            mmv = pe.n
            for i in range(4):
                fc = g2 * 4 + i
                P.op("se", lambda e, sl=slot, i=i, fc=fc: e.copy(
                    out=self.pooledfm[:, fc, :],
                    in_=ps_bt[sl][:, i * 128:i * 128 + 64]),
                    waits=[(pe, mmv)], inc=se if i == 3 else None)
            rot_bt.release(slot, se, se.n)
        pfm = se.n
        for mf in range(2):
            def ev(slot, mmv, mf=mf):
                P.op("se", lambda e, sl=slot, mf=mf: e.activation(
                    out=self.rkfm[:, mf, :], in_=ps_pr[sl][:, 0:64],
                    func=AF.Identity, bias=self.rkvbk[:, mf:mf + 1]),
                    waits=[(pe, mmv)], inc=se)
                rot_pr.release(slot, se, se.n)
            proj(lambda kk, mf=mf: rkvw[:, kk, mf * 128:(mf + 1) * 128],
                 lambda kk: self.pooledfm[:, kk, :], 8, ev,
                 wait_first=[(se, pfm), self.tk_rkvw], ncols=64)

        def ev_rv(slot, mmv):
            P.op("ve", lambda e, sl=slot: e.tensor_tensor(
                out=self.rvlm[:], in0=ps_pr[sl][0:64, 0:256], in1=self.rvb[:],
                op=OP.add), waits=[(pe, mmv)], inc=ve)
            rot_pr.release(slot, ve, ve.n)
        proj(lambda kk: self.pooledfm[:, kk, :],
             lambda kk: rkvw[:, kk, 256:512], 8, ev_rv, ncols=256, nrows=64)
        self._rv_ve = ve.n

        P.wait("sy", self.tk_locsp)
        P.emit(nc, "p0")
        c0.close()

    def phase1(self):
        nc, d, sm = self.nc, self.d, self.sem
        pe, ve, se = (sm[k] for k in ("pe", "ve", "se"))
        self.ce = ExitStack()
        self.gw_sb = self.ce.enter_context(
            nc.sbuf_tensor("gw_sb", [128, 16, 8, 256], BF16))

        c1 = ExitStack()
        S = lambda n, s, t: c1.enter_context(nc.sbuf_tensor(n, s, t))
        PS = lambda n, s, t: c1.enter_context(nc.psum_tensor(n, s, t))
        localre = S("localre", [128, 2, 8, 512], BF16)
        remote = S("remote", [128, 8, 512], BF16)
        latout = S("latout", [128, 2, 512], BF16)
        mixed = S("mixed", [128, 8, 512], BF16)
        lmaskb = S("lmaskb", [128, 2, 128], F32)
        lp_b = S("lp_b", [128, 2, 128], BF16)
        lpn_b = S("lpn_b", [128, 2, 128], BF16)
        lpT_b = S("lpT_b", [64, 2, 256], BF16)
        lmacc = S("lmacc", [128, 4], F32)
        lmn = S("lmn", [128, 4], F32)
        lssum = S("lssum", [128, 4], F32)
        lrvv = S("lrvv", [128, 4], F32)
        tf_sb = S("tf_sb", [2, 2, 512], BF16)
        d_bf = S("d_bf", [128, 512], BF16)
        s_bf = S("s_bf", [128, 512], BF16)
        pm_sb = S("pm_sb", [128, 512], F32)
        xres = S("xres", [128, 4, 1024], F32)
        xob = S("xob", [128, 1024], F32)
        y0b = S("y0b", [128, 2, 1024], F32)
        stg = S("stg", [128, 2, 2048], BF16)
        ps_gen = [PS(f"ps_g1{i}", [128, 512], F32) for i in range(3)]
        ps_bt = [PS(f"ps_b1{i}", [128, 512], BF16) for i in range(2)]
        ps_out = PS("ps_out1", [128, 1024], F32)

        P = Plan()
        st_gw = self.new_stream("gw", 1)
        for ge4 in range(4):
            self.tk_gw = st_gw.dma(
                P, "ge", self.gw_sb[:, ge4 * 4:(ge4 + 1) * 4],
                d["gate_w"][ge4 * 4:(ge4 + 1) * 4].rearrange(
                    "e (n p) m -> p e n m", p=128))
        st_pcg = self.new_stream("pcg", 2)
        st_pcs = [self.new_stream(f"pcs{i}", 1) for i in range(2)]
        stg_rel = [None] * 2
        pcs_tok = [None] * 2
        pcj = [0]

        def precast(src_ap, dst_ap, npc):
            sl = pcj[0] % 2
            pcj[0] += 1
            w = [stg_rel[sl]] if stg_rel[sl] is not None else []
            tkl = st_pcg.dma(P, "ge",
                             stg[:, sl, :].rearrange("p (n m) -> p n m", n=npc),
                             src_ap, waits=w)
            tks = st_pcs[sl].dma(P, "ge", dst_ap,
                                 stg[:, sl, :].rearrange("p (n m) -> p n m", n=npc),
                                 waits=[tkl])
            stg_rel[sl] = tks
            pcs_tok[sl] = tks
            return tks

        for e_i in range(16):
            precast(R3(d["down_w"][e_i]), d["dwbf_sp"][e_i], 2)
        for e_i in range(16):
            precast(R3(d["up_w"][e_i]), d["uwbf_sp"][e_i], 8)
        self.tk_precast_ud = list(pcs_tok)
        for srcw, dstw in [(d["sgw_t"], d["swg_sp"]), (d["suw_t"], d["swu_sp"])]:
            for pc in range(8):
                precast(R3(srcw)[:, pc:pc + 1, :], dstw[:, pc:pc + 1, :], 1)
        for pc in range(8):
            precast(R3(d["sdw_t"])[:, 2 * pc:2 * pc + 2, :],
                    d["swd_sp"][:, 2 * pc:2 * pc + 2, :], 2)
        self.tk_precast = list(pcs_tok)

        rot_g = Rot(3)
        rot_bt1 = Rot(2)
        rel_lmask = [None, None]
        rel_lp = [None, None]
        rel_lpn = [None, None]
        rel_lpT = [None, None]
        rel_tf = [None, None]
        y0_rel = [None, None]
        psout_rel = None

        st_loc = self.new_stream("lre", 4)
        st_xres = self.new_stream("xres", 4)
        st_y0sp = self.new_stream("y0sp", 2)
        loc_rel = [None, None]
        xres_rel4 = [None] * 4
        xob_rel = [None]

        def emit_loc_load(c2):
            lsl2 = c2 % 2
            wl = [self.tk_locsp]
            if loc_rel[lsl2] is not None:
                wl.append(loc_rel[lsl2])
            a = st_loc.dma(P, "sy", localre[:, lsl2, 0:4, :],
                           d["local_sp"][:, 0:4, c2 * 512:(c2 + 1) * 512],
                           waits=wl)
            b = st_loc.dma(P, "sy", localre[:, lsl2, 4:8, :],
                           d["local_sp"][:, 4:8, c2 * 512:(c2 + 1) * 512],
                           waits=wl)
            return [a, b]

        loc_toks = {0: emit_loc_load(0)}
        for c in range(NCH):
            lsl = c % 2
            ld_loc = loc_toks[c]
            if c + 1 < NCH:
                loc_toks[c + 1] = emit_loc_load(c + 1)
            ldx_t = {}
            for mt in range(4):
                w = [xres_rel4[mt]] if xres_rel4[mt] is not None else []
                ldx_t[mt] = st_xres.dma(
                    P, "sy", xres[:, mt, :],
                    d["x_own"][(c * 4 + mt) * 128:(c * 4 + mt + 1) * 128, :],
                    waits=w)

            for qt in range(4):
                gq = 4 * c + qt
                qs = qt * 128
                for fh in range(2):
                    mb = fh
                    slotA, wA = rot_g.acquire()
                    slotB, wB = rot_g.acquire()
                    w = list(wA) + list(wB)
                    for hh in range(2):
                        off = 64 * hh
                        sl_h = slotA if hh == 0 else slotB
                        P.op("te", lambda e, sl=sl_h, fh=fh, hh=hh, off=off,
                             gq=gq: e.matmul(
                                 ps_gen[sl][:, 0:64],
                                 self.rq_sb[off:off + 64, fh,
                                            gq * 128:(gq + 1) * 128],
                                 self.rkfm[off:off + 64, fh, :],
                                 start=True, stop=True),
                             waits=w if hh == 0 else (),
                             inc=pe if hh == 1 else None)
                        w = ()
                    vmm = pe.n
                    wv = [(pe, vmm), self.tk_consts]
                    if rel_lmask[mb] is not None:
                        wv.append(rel_lmask[mb])
                    for hh in range(2):
                        sl_h = slotA if hh == 0 else slotB
                        P.op("ve", lambda e, sl=sl_h, mb=mb, hh=hh, gq=gq:
                             e.tensor_tensor(
                                 out=lmaskb[:, mb, hh * 64:(hh + 1) * 64],
                                 in0=ps_gen[sl][:, 0:64],
                                 in1=self.latmask[:, gq, :], op=OP.add),
                             waits=wv if hh == 0 else ())
                        wv = ()
                    P.drain("ve")
                    lb3 = lmaskb[:, mb, :].rearrange("p (s x) -> p s x", s=2)
                    P.op("ve", lambda e, fh=fh, lb3=lb3: e.tensor_reduce(
                        out=lmn[:, 2 * fh:2 * fh + 2], in_=lb3, axis=AXX,
                        op=OP.max, negate=True))
                    P.drain("ve")
                    P.op("ve", lambda e, fh=fh: e.tensor_scalar(
                        out=lmn[:, 2 * fh:2 * fh + 2],
                        in0=lmn[:, 2 * fh:2 * fh + 2],
                        scalar1=0.0, scalar2=None, op0=OP.min, op1=OP.bypass),
                        inc=ve)
                    vttr = ve.n
                    rot_g.release(slotA, ve, ve.n)
                    rot_g.release(slotB, ve, ve.n)
                    ws = [(ve, vttr)]
                    if rel_lp[mb] is not None:
                        ws.append(rel_lp[mb])
                    for hh in range(2):
                        P.op("se", lambda e, mb=mb, hh=hh, fh=fh: e.activation(
                            out=lp_b[:, mb, hh * 64:(hh + 1) * 64],
                            in_=lmaskb[:, mb, hh * 64:(hh + 1) * 64], func=AF.Exp,
                            bias=lmn[:, 2 * fh + hh:2 * fh + hh + 1],
                            accum_out=lssum[:, 2 * fh + hh:2 * fh + hh + 1]),
                            waits=ws if hh == 0 else (),
                            inc=se if hh == 1 else None)
                        ws = ()
                    vexp = se.n
                    rel_lmask[mb] = (se, se.n)
                    P.op("ve", lambda e, fh=fh: e.tensor_scalar(
                        out=lrvv[:, 2 * fh:2 * fh + 2],
                        in0=lssum[:, 2 * fh:2 * fh + 2],
                        scalar1=1e-20, scalar2=None, op0=OP.add, op1=OP.bypass),
                        waits=[(se, vexp)])
                    P.drain("ve")
                    P.op("ve", lambda e, fh=fh: e.reciprocal(
                        out=lrvv[:, 2 * fh:2 * fh + 2],
                        in_=lrvv[:, 2 * fh:2 * fh + 2]))
                    P.drain("ve")
                    wn = []
                    if rel_lpn[mb] is not None:
                        wn.append(rel_lpn[mb])
                    for hh in range(2):
                        P.op("ve", lambda e, mb=mb, hh=hh, fh=fh: e.tensor_scalar(
                            out=lpn_b[:, mb, hh * 64:(hh + 1) * 64],
                            in0=lp_b[:, mb, hh * 64:(hh + 1) * 64],
                            scalar1=lrvv[:, 2 * fh + hh:2 * fh + hh + 1],
                            scalar2=None, op0=OP.mult, op1=OP.bypass),
                            waits=wn if hh == 0 else (),
                            inc=ve if hh == 1 else None)
                        wn = ()
                    vnrm = ve.n
                    rel_lp[mb] = (ve, ve.n)
                    slot2, w2 = rot_bt1.acquire()
                    w2 = list(w2) + [(ve, vnrm)]
                    for hh in range(2):
                        P.op("te", lambda e, sl=slot2, mb=mb, hh=hh: e.transpose(
                            ps_bt[sl][0:64, hh * 128:(hh + 1) * 128],
                            lpn_b[:, mb, hh * 64:(hh + 1) * 64], self.ident[:]),
                            waits=w2 if hh == 0 else (), inc=pe if hh == 1 else None)
                        w2 = ()
                    rel_lpn[mb] = (pe, pe.n)
                    wvv = [(pe, pe.n)]
                    if rel_lpT[mb] is not None:
                        wvv.append(rel_lpT[mb])
                    P.op("se", lambda e, sl=slot2, mb=mb: e.copy(
                        out=lpT_b[0:64, mb, :], in_=ps_bt[sl][0:64, 0:256]),
                        waits=wvv, inc=se)
                    vevT = se.n
                    rot_bt1.release(slot2, se, se.n)
                    slot3, w3 = rot_g.acquire()
                    w3 = list(w3) + [(se, vevT), (ve, self._rv_ve)]
                    for hh in range(2):
                        h4 = 2 * fh + hh
                        P.op("te", lambda e, sl=slot3, mb=mb, hh=hh, h4=h4:
                             e.matmul(ps_gen[sl][64 * hh:64 * hh + 64, 0:128],
                                      self.rvlm[0:64, h4 * 64:(h4 + 1) * 64],
                                      lpT_b[0:64, mb, hh * 128:(hh + 1) * 128],
                                      start=True, stop=True),
                             waits=w3 if hh == 0 else (),
                             inc=pe if hh == 1 else None)
                        w3 = ()
                    rel_lpT[mb] = (pe, pe.n)
                    P.op("se", lambda e, sl=slot3, fh=fh, qs=qs: e.copy(
                        out=latout[:, fh, qs:qs + 128], in_=ps_gen[sl][:, 0:128]),
                        waits=[(pe, pe.n)], inc=se)
                    rot_g.release(slot3, se, se.n)
            lat_se = se.n

            for of in range(8):
                slot, w = rot_g.acquire()
                w = list(w) + [(se, lat_se), self.tk_routw]
                for lc in range(2):
                    P.op("te", lambda e, sl=slot, of=of, lc=lc: e.matmul(
                        ps_gen[sl][:], self.routw[:, lc, of * 128:(of + 1) * 128],
                        latout[:, lc, :], start=(lc == 0), stop=(lc == 1)),
                        waits=w if lc == 0 else (), inc=pe if lc == 1 else None)
                    w = ()
                vmm = pe.n
                P.op("se", lambda e, sl=slot, of=of: e.activation(
                    out=remote[:, of, :], in_=ps_gen[sl][:], func=AF.Identity,
                    bias=self.routbt[:, of:of + 1]), waits=[(pe, vmm)], inc=se)
                rot_g.release(slot, se, se.n)
            rem_se = se.n

            for f in range(8):
                fb = f % 2
                slot, w = rot_g.acquire()
                w = list(w) + [(se, rem_se), self.tk_gec] + ld_loc
                P.op("te", lambda e, sl=slot, f=f, lsl=lsl: e.matmul(
                    ps_gen[sl][0:2, :], self.glgr[:, 0:2], localre[:, lsl, f, :],
                    start=True, stop=False), waits=w)
                P.op("te", lambda e, sl=slot, f=f: e.matmul(
                    ps_gen[sl][0:2, :], self.glgr[:, 2:4], remote[:, f, :],
                    start=False, stop=True), inc=pe)
                vz = pe.n
                wt = [(pe, vz)]
                if rel_tf[fb] is not None:
                    wt.append(rel_tf[fb])
                P.op("se", lambda e, sl=slot, fb=fb: e.activation(
                    out=tf_sb[0:2, fb, :], in_=ps_gen[sl][0:2, :], func=AF.Tanh,
                    scale=0.5), waits=wt, inc=se)
                vth = se.n
                rot_g.release(slot, se, se.n)
                slot2, w2 = rot_g.acquire()
                P.op("te", lambda e, sl=slot2, fb=fb: e.matmul(
                    ps_gen[sl][:], self.rep2[:], tf_sb[0:2, fb, :],
                    start=True, stop=True),
                    waits=list(w2) + [(se, vth)], inc=pe)
                vtr = pe.n
                rel_tf[fb] = (pe, pe.n)
                P.op("ve", lambda e, f=f, lsl=lsl: e.tensor_tensor(
                    out=d_bf[:], in0=localre[:, lsl, f, :], in1=remote[:, f, :],
                    op=OP.subtract), waits=[(pe, vtr)])
                P.op("ve", lambda e, f=f, lsl=lsl: e.tensor_tensor(
                    out=s_bf[:], in0=localre[:, lsl, f, :], in1=remote[:, f, :],
                    op=OP.add))
                P.drain("ve")
                P.op("ve", lambda e, sl=slot2: e.tensor_tensor(
                    out=pm_sb[:], in0=d_bf[:], in1=ps_gen[sl][:], op=OP.mult),
                    inc=ve)
                rot_g.release(slot2, ve, ve.n)
                P.drain("ve")
                P.op("ve", lambda e, f=f: e.scalar_tensor_tensor(
                    out=mixed[:, f, :], in0=s_bf[:], scalar=0.5, in1=pm_sb[:],
                    op0=OP.mult, op1=OP.add), inc=ve)
                P.drain("ve")
            mix_ve = ve.n
            loc_rel[lsl] = (pe, pe.n)

            for mt in range(4):
                sl = mt % 2
                wxo = [ldx_t[mt], self.tk_consts]
                if xob_rel[0] is not None:
                    wxo.append(xob_rel[0])
                P.op("ve", lambda e, mt=mt: e.tensor_tensor(
                    out=xob[:], in0=xres[:, mt, :], in1=self.outbrep[:],
                    op=OP.add), waits=wxo, inc=ve)
                xres_rel4[mt] = (ve, ve.n)
                vxob = ve.n
                w = [(ve, mix_ve), self.tk_outw]
                if psout_rel is not None:
                    w.append(psout_rel)
                for cc in range(8):
                    for nh in range(2):
                        P.op("te", lambda e, cc=cc, nh=nh, mt=mt: e.matmul(
                            ps_out[:, nh * 512:(nh + 1) * 512],
                            mixed[:, cc, mt * 128:(mt + 1) * 128],
                            self.outw[:, cc, nh * 512:(nh + 1) * 512],
                            start=(cc == 0), stop=(cc == 7)),
                            waits=w if (cc == 0 and nh == 0) else (),
                            inc=pe if (cc == 7 and nh == 1) else None)
                        w = ()
                vo = pe.n
                wy = [(pe, vo), (ve, vxob)]
                if y0_rel[sl] is not None:
                    wy.append(y0_rel[sl])
                P.op("ve", lambda e, sl=sl: e.tensor_tensor(
                    out=y0b[:, sl, :], in0=ps_out[:], in1=xob[:],
                    op=OP.add), waits=wy, inc=ve)
                xob_rel[0] = (ve, ve.n)
                psout_rel = (ve, ve.n)
                vy0 = ve.n
                self.tk_y0sp = st_y0sp.dma(P, "sy", d["y0_sp"][c * 4 + mt],
                                           y0b[:, sl, :], waits=[(ve, vy0)])
                y0_rel[sl] = self.tk_y0sp

        self.tk_y0sp_all = [(s, s.n) for s in st_y0sp.sems]
        P.wait("sy", *self.tk_y0sp_all)
        P.emit(nc, "p1")
        c1.close()

    def phase15(self):
        nc, d, sm = self.nc, self.d, self.sem
        pe, ve, se = (sm[k] for k in ("pe", "ve", "se"))
        c2 = ExitStack()
        S = lambda n, s, t: c2.enter_context(nc.sbuf_tensor(n, s, t))
        PS = lambda n, s, t: c2.enter_context(nc.psum_tensor(n, s, t))
        y0r = S("y0r", [128, 4, 1024], F32)
        bnst = S("bnst", [128, 4, 12], F32)
        mv = S("mv", [128, 4, 2], F32)
        lnv = S("lnv", [128, 4], F32)
        rstd = S("rstd", [128, 4], F32)
        nmr = S("nmr", [128, 4], F32)
        y1f = S("y1f", [128, 4, 1024], F32)
        y1bf = S("y1bf", [128, 2, 1024], BF16)
        y1stage = S("y1stage", [128, 2, 4, 128], BF16)
        sco_all = S("sco_all", [128, 16, 16], F32)
        enl_all = S("enl_all", [128, 16, 16], F32)
        scor_all = S("scor_all", [128, 16, 16], F32)
        sb_all = S("sb_all", [128, 16, 16], F32)
        prs_all = S("prs_all", [128, 16, 6, 4], F32)
        gsum_all = S("gsum_all", [128, 16, 4], F32)
        cge_all = S("cge_all", [128, 16, 4], F32)
        cgt_all = S("cgt_all", [128, 16, 4], F32)
        grank_all = S("grank_all", [128, 16, 4], F32)
        gmask_all = S("gmask_all", [128, 16, 4], F32)
        maskd_all = S("maskd_all", [128, 16, 16], F32)
        maskd0_all = S("maskd0_all", [128, 16, 16], F32)
        mk_all = S("mk_all", [128, 16], F32)
        tm_all = S("tm_all", [128, 16], F32)
        eq_all = S("eq_all", [128, 16, 16], F32)
        keep_all = S("keep_all", [128, 16, 16], F32)
        wts_all = S("wts_all", [128, 16, 16], F32)
        wsum_all = S("wsum_all", [128, 16], F32)
        comb_all = S("comb_all", [128, 16, 16], BF16)
        ps_bt = [PS(f"ps_b2{i}", [128, 512], BF16) for i in range(2)]
        ps_lg = [PS(f"ps_lg{i}", [128, 16], F32) for i in range(2)]
        ps_ct = [PS(f"ps_ct{i}", [16, 512], BF16) for i in range(2)]

        P = Plan()
        rot_bt2 = Rot(2)
        y0r_rel = [None] * 4
        y1f_rel = [None] * 4
        lg_rel = [None, None]
        st_y0r = self.new_stream("y0r", 8)
        st_y1sp = self.new_stream("y1sp", 8)
        st_y1fmsp = self.new_stream("yfsp", 4)

        vmv_a = [0] * 16
        vrs_a = [0] * 16
        vnm_a = [0] * 16
        vy1b_a = [0] * 16
        mv_rel = [None] * 4
        nmr_rel = [None] * 4
        y1bf_rel = [None, None]

        def L1(t):
            sl = t % 4
            w = list(self.tk_y0sp_all)
            if y0r_rel[sl] is not None:
                w.append(y0r_rel[sl])
            ldt0 = st_y0r.dma(P, "sy", y0r[:, sl, 0:512],
                              d["y0_sp"][t][:, 0:512], waits=w)
            ldt1 = st_y0r.dma(P, "sy", y0r[:, sl, 512:1024],
                              d["y0_sp"][t][:, 512:1024], waits=w)
            P.op("ve", lambda e, sl=sl: e.bn_stats(out=bnst[:, sl, 0:6],
                                                   in_=y0r[:, sl, 0:512]),
                 waits=[ldt0])
            P.op("ve", lambda e, sl=sl: e.bn_stats(out=bnst[:, sl, 6:12],
                                                   in_=y0r[:, sl, 512:1024]),
                 waits=[ldt1])
            P.drain("ve")
            wagg = [mv_rel[sl]] if mv_rel[sl] is not None else []
            P.op("ve", lambda e, sl=sl: e.bn_aggr(out=mv[:, sl, :],
                                                  in_=bnst[:, sl, :]),
                 waits=wagg, inc=ve)
            vmv_a[t] = ve.n

        def L2(t):
            sl = t % 4
            P.op("se", lambda e, sl=sl: e.activation(
                out=lnv[:, sl:sl + 1], in_=mv[:, sl, 1:2],
                func=AF.Ln, bias=self.epsv[:, 0:1]),
                waits=[(ve, vmv_a[t]), self.tk_eps])
            P.drain("se")
            P.op("se", lambda e, sl=sl: e.activation(
                out=rstd[:, sl:sl + 1], in_=lnv[:, sl:sl + 1],
                func=AF.Exp, scale=-0.5), inc=se)
            vrs_a[t] = se.n
            mv_rel[sl] = (se, se.n)

        def L3(t):
            sl = t % 4
            w = [(se, vrs_a[t])]
            if nmr_rel[sl] is not None:
                w.append(nmr_rel[sl])
            P.op("ve", lambda e, sl=sl: e.tensor_scalar(
                out=nmr[:, sl:sl + 1], in0=mv[:, sl, 0:1],
                scalar1=rstd[:, sl:sl + 1], scalar2=-1.0,
                op0=OP.mult, op1=OP.mult), waits=w, inc=ve)
            vnm_a[t] = ve.n

        def L4(t):
            sl = t % 4
            sb2 = t % 2
            wy1 = [(ve, vnm_a[t])]
            if y1f_rel[sl] is not None:
                wy1 += y1f_rel[sl]
            P.op("se", lambda e, sl=sl: e.activation(
                out=y1f[:, sl, :], in_=y0r[:, sl, :], func=AF.Identity,
                scale=rstd[:, sl:sl + 1], bias=nmr[:, sl:sl + 1]),
                waits=wy1, inc=se)
            y0r_rel[sl] = (se, se.n)
            nmr_rel[sl] = (se, se.n)
            P.drain("se")
            wbf = [y1bf_rel[sb2]] if y1bf_rel[sb2] is not None else []
            P.op("se", lambda e, sl=sl, sb2=sb2: e.copy(
                out=y1bf[:, sb2, :], in_=y1f[:, sl, :]), waits=wbf, inc=se)
            vy1b_a[t] = se.n
            tks0 = st_y1sp.dma(P, "sy", d["y1_sp"][t][:, 0:512],
                               y1f[:, sl, 0:512], waits=[(se, vy1b_a[t])])
            tks1 = st_y1sp.dma(P, "sy", d["y1_sp"][t][:, 512:1024],
                               y1f[:, sl, 512:1024], waits=[(se, vy1b_a[t])])
            y1f_rel[sl] = [tks0, tks1]

        def L5(t):
            sb2 = t % 2
            yst_w = [(s, s.n) for s in st_y1fmsp.sems] if t > 0 else []
            if self.p15_mode < 1:
                return
            for g2 in range(2):
                slot, w2 = rot_bt2.acquire()
                w2 = list(w2) + [(se, vy1b_a[t])] + yst_w
                for i in range(4):
                    fc = g2 * 4 + i
                    P.op("te", lambda e, sl2=slot, i=i, fc=fc, sb2=sb2:
                         e.transpose(
                             ps_bt[sl2][:, i * 128:(i + 1) * 128],
                             y1bf[:, sb2, fc * 128:(fc + 1) * 128],
                             self.ident[:]),
                         waits=w2 if i == 0 else (), inc=pe if i == 3 else None)
                    w2 = ()
                vtr = pe.n
                P.op("se", lambda e, sl2=slot, g2=g2: e.copy(
                    out=y1stage[:, g2].rearrange("p n m -> p (n m)"),
                    in_=ps_bt[sl2][:]), waits=[(pe, vtr)], inc=se)
                rot_bt2.release(slot, se, se.n)
            y1bf_rel[sb2] = (pe, pe.n)
            vstage = se.n
            st_y1fmsp.dma(P, "sy", d["y1fm_sp"][:, 0:4, t * 128:(t + 1) * 128],
                          y1stage[:, 0], waits=[(se, vstage)])
            self.tk_yfsp = st_y1fmsp.dma(
                P, "sy", d["y1fm_sp"][:, 4:8, t * 128:(t + 1) * 128],
                y1stage[:, 1], waits=[(se, vstage)])
            if self.p15_mode < 2:
                return
            slg = t % 2
            wlg = [(se, vstage), self.tk_routerw]
            if lg_rel[slg] is not None:
                wlg.append(lg_rel[slg])
            for cc in range(8):
                P.op("te", lambda e, slg=slg, cc=cc: e.matmul(
                    ps_lg[slg][:], y1stage[:, cc // 4, cc % 4, :],
                    self.routerw[:, cc, :], start=(cc == 0), stop=(cc == 7)),
                    waits=wlg if cc == 0 else (),
                    inc=pe if cc == 7 else None)
            vlg = pe.n
            P.op("se", lambda e, slg=slg, t=t: e.copy(
                out=sco_all[:, t, :], in_=ps_lg[slg][:]),
                waits=[(pe, vlg)], inc=se)
            lg_rel[slg] = (se, se.n)

        for i in range(16 + 4):
            if i < 16:
                L1(i)
            if 0 <= i - 1 < 16:
                L2(i - 1)
            if 0 <= i - 2 < 16:
                L3(i - 2)
            if 0 <= i - 3 < 16:
                L4(i - 3)
            if 0 <= i - 4 < 16:
                L5(i - 4)

        if self.p15_mode >= 3:
            v_sco = se.n
            P.drain("se")
            P.op("se", lambda e: e.activation(
                out=enl_all[:].rearrange("p a b -> p (a b)"),
                in_=sco_all[:].rearrange("p a b -> p (a b)"),
                func=AF.Exp, scale=-1.0), inc=se)
            vexp = se.n
            P.op("ve", lambda e: e.tensor_scalar(
                out=scor_all[:], in0=enl_all[:], scalar1=1.0, scalar2=None,
                op0=OP.add, op1=OP.bypass), waits=[(se, vexp), self.tk_consts])
            P.drain("ve")
            P.op("ve", lambda e: e.reciprocal(
                out=scor_all[:].rearrange("p a b -> p (a b)"),
                in_=scor_all[:].rearrange("p a b -> p (a b)")))
            P.drain("ve")
            corr_bc = self.corr[:].unsqueeze(1).broadcast_to([128, 16, 16])
            P.op("ve", lambda e, corr_bc=corr_bc: e.tensor_tensor(
                out=sb_all[:], in0=scor_all[:], in1=corr_bc, op=OP.add))
            P.drain("ve")
            sb4 = sb_all[:].rearrange("p t (g e) -> p t g e", e=4)
            for pi, (i_, j_) in enumerate([(0, 1), (0, 2), (0, 3), (1, 2),
                                           (1, 3), (2, 3)]):
                P.op("ve", lambda e, pi=pi, i_=i_, j_=j_, sb4=sb4:
                     e.tensor_tensor(
                         out=prs_all[:, :, pi, :], in0=sb4[:, :, :, i_],
                         in1=sb4[:, :, :, j_], op=OP.add))
            P.drain("ve")
            P.op("ve", lambda e: e.tensor_tensor(
                out=gsum_all[:], in0=prs_all[:, :, 0, :],
                in1=prs_all[:, :, 1, :], op=OP.max))
            P.drain("ve")
            for pi in range(2, 6):
                P.op("ve", lambda e, pi=pi: e.tensor_tensor(
                    out=gsum_all[:], in0=gsum_all[:], in1=prs_all[:, :, pi, :],
                    op=OP.max))
                P.drain("ve")
            for gi in range(4):
                gs_bc = gsum_all[:, :, gi:gi + 1].broadcast_to([128, 16, 4])
                P.op("ve", lambda e, gs_bc=gs_bc: e.tensor_tensor(
                    out=cge_all[:], in0=gsum_all[:], in1=gs_bc, op=OP.is_ge))
                P.op("ve", lambda e, gs_bc=gs_bc: e.tensor_tensor(
                    out=cgt_all[:], in0=gsum_all[:], in1=gs_bc, op=OP.is_gt))
                P.drain("ve")
                if gi == 0:
                    P.op("ve", lambda e: e.tensor_reduce(
                        out=grank_all[:, :, 0:1], in_=cgt_all[:, :, 1:4],
                        axis=AXX, op=OP.add))
                elif gi == 3:
                    P.op("ve", lambda e: e.tensor_reduce(
                        out=grank_all[:, :, 3:4], in_=cge_all[:, :, 0:3],
                        axis=AXX, op=OP.add))
                else:
                    P.op("ve", lambda e, gi=gi: e.tensor_reduce(
                        out=grank_all[:, :, gi:gi + 1], in_=cge_all[:, :, 0:gi],
                        axis=AXX, op=OP.add))
                    P.op("ve", lambda e, gi=gi: e.tensor_reduce(
                        out=tm_all[:], in_=cgt_all[:, :, gi + 1:4],
                        axis=AXX, op=OP.add))
                    P.drain("ve")
                    P.op("ve", lambda e, gi=gi: e.tensor_tensor(
                        out=grank_all[:, :, gi:gi + 1],
                        in0=grank_all[:, :, gi:gi + 1],
                        in1=tm_all[:].unsqueeze(2), op=OP.add))
                P.drain("ve")
            P.op("ve", lambda e: e.tensor_scalar(
                out=gmask_all[:], in0=grank_all[:], scalar1=1.5, scalar2=None,
                op0=OP.is_le, op1=OP.bypass))
            P.drain("ve")
            for g in range(4):
                gm_bc = gmask_all[:, :, g:g + 1].broadcast_to([128, 16, 4])
                P.op("ve", lambda e, g=g, gm_bc=gm_bc: e.tensor_tensor(
                    out=maskd_all[:, :, g * 4:(g + 1) * 4],
                    in0=sb_all[:, :, g * 4:(g + 1) * 4], in1=gm_bc,
                    op=OP.mult))
            P.drain("ve")
            P.op("ve", lambda e: e.tensor_scalar(
                out=gmask_all[:], in0=gmask_all[:], scalar1=1e5, scalar2=-1e5,
                op0=OP.mult, op1=OP.add))
            P.drain("ve")
            for g in range(4):
                gm_bc = gmask_all[:, :, g:g + 1].broadcast_to([128, 16, 4])
                P.op("ve", lambda e, g=g, gm_bc=gm_bc: e.tensor_tensor(
                    out=maskd_all[:, :, g * 4:(g + 1) * 4],
                    in0=maskd_all[:, :, g * 4:(g + 1) * 4], in1=gm_bc,
                    op=OP.add))
            P.drain("ve")
            P.op("ve", lambda e: e.tensor_copy(
                out=maskd0_all[:].rearrange("p a b -> p (a b)"),
                in_=maskd_all[:].rearrange("p a b -> p (a b)")))
            P.drain("ve")
            mk_bc = mk_all[:].unsqueeze(2).broadcast_to([128, 16, 16])
            for kk in range(4):
                P.op("ve", lambda e: e.tensor_reduce(
                    out=mk_all[:], in_=maskd_all[:], axis=AXX, op=OP.max))
                P.drain("ve")
                if kk < 3:
                    P.op("ve", lambda e, mk_bc=mk_bc: e.tensor_tensor(
                        out=eq_all[:], in0=maskd_all[:], in1=mk_bc,
                        op=OP.is_ge))
                    P.drain("ve")
                    P.op("ve", lambda e: e.scalar_tensor_tensor(
                        out=maskd_all[:], in0=eq_all[:], scalar=-3e5,
                        in1=maskd_all[:], op0=OP.mult, op1=OP.add))
                    P.drain("ve")
            P.op("ve", lambda e, mk_bc=mk_bc: e.tensor_tensor(
                out=keep_all[:], in0=maskd0_all[:], in1=mk_bc, op=OP.is_ge))
            P.drain("ve")
            P.op("ve", lambda e: e.tensor_tensor(
                out=wts_all[:], in0=keep_all[:], in1=scor_all[:], op=OP.mult))
            P.drain("ve")
            P.op("ve", lambda e: e.tensor_reduce(
                out=wsum_all[:], in_=wts_all[:], axis=AXX, op=OP.add))
            P.drain("ve")
            P.op("ve", lambda e: e.tensor_scalar(
                out=wsum_all[:], in0=wsum_all[:], scalar1=1e-20, scalar2=None,
                op0=OP.add, op1=OP.bypass))
            P.drain("ve")
            P.op("ve", lambda e: e.reciprocal(out=wsum_all[:], in_=wsum_all[:]))
            P.drain("ve")
            ws_bc = wsum_all[:].unsqueeze(2).broadcast_to([128, 16, 16])
            P.op("ve", lambda e, ws_bc=ws_bc: e.tensor_tensor(
                out=comb_all[:], in0=wts_all[:], in1=ws_bc, op=OP.mult),
                inc=ve)
            vcomb = ve.n
            ct_rel = [None, None]
            for q in range(4):
                slc = q % 2
                wct = [(ve, vcomb)]
                if ct_rel[slc] is not None:
                    wct.append(ct_rel[slc])
                for i in range(4):
                    P.op("te", lambda e, slc=slc, q=q, i=i: e.transpose(
                        ps_ct[slc][0:16, i * 128:(i + 1) * 128],
                        comb_all[:, q * 4 + i, :], self.ident[:]),
                        waits=wct if i == 0 else (),
                        inc=pe if i == 3 else None)
                vct = pe.n
                P.op("se", lambda e, slc=slc, q=q: e.copy(
                    out=self.combT[0:16, q * 512:(q + 1) * 512],
                    in_=ps_ct[slc][0:16, :]), waits=[(pe, vct)], inc=se)
                ct_rel[slc] = (se, se.n)

        self.tk_y1sp_all = [(s, s.n) for s in st_y1sp.sems]
        self.tk_yfsp_all = [(s, s.n) for s in st_y1fmsp.sems]
        P.wait("sy", *(self.tk_y1sp_all + self.tk_yfsp_all))
        P.emit(nc, "p15")
        self._p15_comb_se = se.n
        c2.close()

    def phase2(self):
        nc, d, sm = self.nc, self.d, self.sem
        pe, ve, se = (sm[k] for k in ("pe", "ve", "se"))
        c3 = ExitStack()
        S = lambda n, s, t: c3.enter_context(nc.sbuf_tensor(n, s, t))
        PS = lambda n, s, t: c3.enter_context(nc.psum_tensor(n, s, t))
        y1s = S("y1s", [128, 2, 8, 256], BF16)
        uwb = S("uwb", [128, 6, 8, 256], BF16)
        dwb = S("dwb", [128, 6, 2, 1024], BF16)
        sg_sb = S("sg_sb", [128, 2, 512], BF16)
        h_sb = S("h_sb", [128, 512], BF16)
        hwst = S("hwst", [128, 2, 512], BF16)
        y1r = S("y1r", [128, 2, 1024], F32)
        ry = S("ry", [128, 2, 1024], F32)
        ps_rt = [PS(f"ps_rt{i}", [128, 1024], F32) for i in range(2)]
        ps_g = PS("ps_g2", [128, 512], F32)
        ps_u = [PS(f"ps_u2{i}", [128, 512], F32) for i in range(2)]
        ps_wb = PS("ps_wb", [128, 256], F32)
        wb_sb = S("wb_sb", [128, 2, 256], F32)

        P = Plan()
        st_y1s = self.new_stream("y1s2", 4)
        st_uw = self.new_stream("uw", 6)
        st_dw = self.new_stream("dw", 6)
        st_y1r = self.new_stream("y1r2", 2)
        st_rysp = self.new_stream("rysp", 2)
        y1s_rel = [None, None]
        uwb_rel = [None] * 6
        dwb_rel = [None] * 6
        y1r_rel = [None, None]
        ry_rel = [None, None]
        wb_rel = [None, None]
        wb_ps_rel = [None]
        psu_rel = [None, None]
        rt_rel = [None, None]
        silu_prev = 0
        h_prev = 0
        h_prev2 = [0, 0]
        prev_down_pe = 0

        def emit_y1s_load(tb):
            sl = tb % 2
            w = list(self.tk_yfsp_all) + list(self.tk_precast)
            if y1s_rel[sl] is not None:
                w.append(y1s_rel[sl])
            a = st_y1s.dma(P, "sy", y1s[:, sl, 0:4, :],
                           d["y1fm_sp"][:, 0:4, tb * 256:(tb + 1) * 256],
                           waits=w)
            b = st_y1s.dma(P, "sy", y1s[:, sl, 4:8, :],
                           d["y1fm_sp"][:, 4:8, tb * 256:(tb + 1) * 256],
                           waits=w)
            return [a, b]

        ldy_toks = {0: emit_y1s_load(0)}
        for tb in range(8):
            sl = tb % 2
            ldy_a, ldy_b = ldy_toks[tb]
            if tb + 1 < 8:
                ldy_toks[tb + 1] = emit_y1s_load(tb + 1)
            hw_ve = [0] * 16
            dwld = [0] * 16
            uwld = [0] * 16
            down_pe = [0] * 17
            for e_i in range(17):
                if e_i < 16:
                    su = (tb * 16 + e_i) % 6
                    wu = [uwb_rel[su]] if uwb_rel[su] is not None else []
                    uwld[e_i] = st_uw.dma(P, "sy", uwb[:, su],
                                          d["uwbf_sp"][e_i],
                                          waits=wu + self.tk_precast)
                    wd = [dwb_rel[su]] if dwb_rel[su] is not None else []
                    dwld[e_i] = st_dw.dma(P, "ge", dwb[:, su], d["dwbf_sp"][e_i],
                                          waits=wd + self.tk_precast)
                    sw = e_i % 2
                    if self.p2_mode >= 1:
                        ww = [(se, self._p15_comb_se), self.tk_gec]
                        if wb_ps_rel[0] is not None:
                            ww.append(wb_ps_rel[0])
                        P.op("te", lambda e, e_i=e_i, tb=tb: e.matmul(
                            ps_wb[:], self.ue16[:, e_i * 128:(e_i + 1) * 128],
                            self.combT[0:16, tb * 256:(tb + 1) * 256],
                            start=True, stop=True), waits=ww, inc=pe)
                        vwb = pe.n
                        wwc = [(pe, vwb)] + list(self.tk_precast)
                        if wb_rel[sw] is not None:
                            wwc.append(wb_rel[sw])
                        P.op("se", lambda e, sw=sw: e.copy(
                            out=wb_sb[:, sw, :], in_=ps_wb[:]), waits=wwc,
                            inc=se)
                        wb_ps_rel[0] = (se, se.n)
                    if self.p2_mode < 2:
                        continue
                    wg = [self.tk_gw, ldy_a, ldy_b, (se, silu_prev)]
                    for mh in range(2):
                        for cc in range(8):
                            P.op("te", lambda e, e_i=e_i, cc=cc, mh=mh, sl=sl:
                                 e.matmul(ps_g[:, mh * 256:(mh + 1) * 256],
                                          self.gw_sb[:, e_i, cc,
                                                     mh * 128:(mh + 1) * 128],
                                          y1s[:, sl, cc, :],
                                          start=(cc == 0), stop=(cc == 7)),
                                 waits=wg if (cc == 0 and mh == 0) else (),
                                 inc=pe if (cc == 7 and mh == 1) else None)
                            wg = ()
                    vg = pe.n
                    su2 = e_i % 2
                    wu2 = [uwld[e_i]]
                    if psu_rel[su2] is not None:
                        wu2.append(psu_rel[su2])
                    for mh in range(2):
                        for cc in range(8):
                            P.op("te", lambda e, su=su, su2=su2, cc=cc, mh=mh,
                                 sl=sl:
                                 e.matmul(ps_u[su2][:, mh * 256:(mh + 1) * 256],
                                          uwb[:, su, cc, mh * 128:(mh + 1) * 128],
                                          y1s[:, sl, cc, :],
                                          start=(cc == 0), stop=(cc == 7)),
                                 waits=wu2 if (cc == 0 and mh == 0) else (),
                                 inc=pe if (cc == 7 and mh == 1) else None)
                            wu2 = ()
                    vu = pe.n
                    uwb_rel[su] = (pe, pe.n)
                if e_i >= 1 and self.p2_mode >= 3:
                    ep = e_i - 1
                    sd = (tb * 16 + ep) % 6
                    wmm = [(ve, hw_ve[ep]), dwld[ep]]
                    if ep == 0:
                        for sr in range(2):
                            if rt_rel[sr] is not None:
                                wmm.append(rt_rel[sr])
                    for mt in range(2):
                        for kc in range(2):
                            for nh in range(2):
                                first = (ep == 0 and kc == 0)
                                last = (ep == 15 and kc == 1)
                                P.op("te", lambda e, sd=sd, ep=ep, mt=mt, kc=kc,
                                     nh=nh, first=first, last=last: e.matmul(
                                         ps_rt[mt][:, nh * 512:(nh + 1) * 512],
                                         hwst[:, ep % 2, kc * 256 + mt * 128:
                                              kc * 256 + mt * 128 + 128],
                                         dwb[:, sd, kc, nh * 512:(nh + 1) * 512],
                                         start=first, stop=last),
                                     waits=wmm if (mt == 0 and kc == 0 and nh == 0)
                                     else (),
                                     inc=pe if (mt == 1 and kc == 1 and nh == 1)
                                     else None)
                                wmm = ()
                    down_pe[ep] = pe.n
                    dwb_rel[sd] = (pe, pe.n)
                if e_i < 16:
                    sgsl = e_i % 2
                    P.op("se", lambda e, sgsl=sgsl: e.activation(
                        out=sg_sb[:, sgsl, :], in_=ps_g[:], func=AF.Silu),
                        waits=[(pe, vg), (ve, h_prev2[sgsl])], inc=se)
                    silu_prev = se.n
                    P.op("ve", lambda e, sgsl=sgsl, su2=su2: e.tensor_tensor(
                        out=h_sb[:], in0=sg_sb[:, sgsl, :], in1=ps_u[su2][:],
                        op=OP.mult),
                        waits=[(se, silu_prev), (pe, vu)], inc=ve)
                    h_prev = ve.n
                    h_prev2[sgsl] = ve.n
                    psu_rel[su2] = (ve, ve.n)
                    P.drain("ve")
                    sw = e_i % 2
                    wb3 = wb_sb[:, sw, :].unsqueeze(1).broadcast_to([128, 2, 256])
                    whw = [(pe, prev_down_pe), (se, silu_prev)]
                    P.op("ve", lambda e, e_i=e_i, wb3=wb3: e.tensor_tensor(
                        out=hwst[:, e_i % 2, :].rearrange("p (n m) -> p n m", n=2),
                        in0=h_sb[:].rearrange("p (n m) -> p n m", n=2),
                        in1=wb3, op=OP.mult), waits=whw, inc=ve)
                    hw_ve[e_i] = ve.n
                    wb_rel[sw] = (ve, ve.n)
                    P.drain("ve")
                    prev_down_pe = down_pe[e_i - 1] if e_i >= 1 else prev_down_pe
            for mt in (range(2) if self.p2_mode >= 4 else []):
                sl2 = mt % 2
                wy = list(self.tk_y1sp_all)
                if y1r_rel[sl2] is not None:
                    wy.append(y1r_rel[sl2])
                ldy1 = st_y1r.dma(P, "sy", y1r[:, sl2, :],
                                  d["y1_sp"][tb * 2 + mt], waits=wy)
                wv = [(pe, down_pe[15]), ldy1]
                if ry_rel[sl2] is not None:
                    wv.append(ry_rel[sl2])
                P.op("ve", lambda e, mt=mt, sl2=sl2: e.tensor_tensor(
                    out=ry[:, sl2, :], in0=ps_rt[mt][:], in1=y1r[:, sl2, :],
                    op=OP.add), waits=wv, inc=ve)
                y1r_rel[sl2] = (ve, ve.n)
                rt_rel[mt] = (ve, ve.n)
                vry = ve.n
                tk = st_rysp.dma(P, "sy", d["routedy_sp"][tb * 2 + mt],
                                 ry[:, sl2, :], waits=[(ve, vry)])
                ry_rel[sl2] = tk
            y1s_rel[sl] = (pe, down_pe[15] if self.p2_mode >= 3 else pe.n)

        self.tk_rysp_all = [(s, s.n) for s in st_rysp.sems]
        P.wait("sy", *self.tk_rysp_all)
        P.emit(nc, "p2")
        c3.close()
        self.ce.close()
        self.ca.close()
        self.ce = None
        self.ca = None

    def phase3(self):
        nc, d, sm = self.nc, self.d, self.sem
        pe, ve, se = (sm[k] for k in ("pe", "ve", "se"))
        c4 = ExitStack()
        S = lambda n, s, t: c4.enter_context(nc.sbuf_tensor(n, s, t))
        PS = lambda n, s, t: c4.enter_context(nc.psum_tensor(n, s, t))
        sgw = S("sgw", [128, 8, 2048], BF16)
        suw = S("suw", [128, 8, 2048], BF16)
        sdw = S("sdw", [128, 16, 1024], BF16)
        y1s = S("y1s3", [128, 3, 8, 256], BF16)
        sgs = S("sgs", [128, 2, 256], BF16)
        h_sb = S("h_sb3", [128, 16, 256], BF16)
        ryr = S("ryr", [128, 4, 1024], F32)
        y2p = S("y2p", [128, 2, 1024], F32)
        bnst = S("bnst5", [128, 2, 12], F32)
        mv = S("mv5", [128, 2, 2], F32)
        lnv = S("lnv5", [128, 2], F32)
        rstd = S("rstd5", [128, 2], F32)
        nmr = S("nmr5", [128, 2], F32)
        ob = S("ob5", [128, 2, 1024], F32)
        ps_d = [PS(f"ps_d3{i}", [128, 1024], F32) for i in range(2)]
        ps_g = [PS(f"ps_g3{i}", [128, 512], F32) for i in range(2)]
        ps_u = [PS(f"ps_u3{i}", [128, 512], F32) for i in range(2)]

        P = Plan()
        st_sw = self.new_stream("ssw", 6)
        for qq in range(4):
            st_sw.dma(P, "sy", sgw[:, qq * 2:qq * 2 + 2, :],
                      d["swg_sp"][:, qq * 2:qq * 2 + 2, :],
                      waits=self.tk_precast)
        for qq in range(4):
            st_sw.dma(P, "sy", suw[:, qq * 2:qq * 2 + 2, :],
                      d["swu_sp"][:, qq * 2:qq * 2 + 2, :],
                      waits=self.tk_precast)
        for qq in range(4):
            st_sw.dma(P, "sy", sdw[:, qq * 4:qq * 4 + 4, :],
                      d["swd_sp"][:, qq * 4:qq * 4 + 4, :],
                      waits=self.tk_precast)
        sw_all = [(s, s.n) for s in st_sw.sems]
        ldsg = sw_all
        ldsu = sw_all
        ldsd = sw_all
        st_y1s3 = self.new_stream("y1s3", 4)
        st_ryr = self.new_stream("ryr", 8)
        st_y2sp = self.new_stream("y2sp", 2)
        st_out = self.new_stream("outs", 2)
        ob_rel = [None, None]
        rot_g = Rot(2)
        rot_u = Rot(2)
        y1s_rel = [None] * 3
        ryr_rel = [None] * 4
        y2p_rel = [None, None]
        psd_rel = [None, None]
        prev_down_pe = 0
        h_prev3 = [0, 0]

        def emit_y1s3_load(tb):
            sl = tb % 3
            w = list(self.tk_yfsp_all)
            if y1s_rel[sl] is not None:
                w.append(y1s_rel[sl])
            a = st_y1s3.dma(P, "sy", y1s[:, sl, 0:4, :],
                            d["y1fm_sp"][:, 0:4, tb * 256:(tb + 1) * 256],
                            waits=w)
            b = st_y1s3.dma(P, "sy", y1s[:, sl, 4:8, :],
                            d["y1fm_sp"][:, 4:8, tb * 256:(tb + 1) * 256],
                            waits=w)
            return [a, b]

        ldy3_toks = {0: emit_y1s3_load(0)}
        for tb in range(8):
            sl = tb % 3
            ldy = ldy3_toks[tb]
            if tb + 1 < 8:
                ldy3_toks[tb + 1] = emit_y1s3_load(tb + 1)
            h_ve = [0] * 16
            for mh in range(16):
                slg, wg = rot_g.acquire()
                wg = list(wg) + ldsg + ldy
                for cc in range(8):
                    P.op("te", lambda e, sl2=slg, mh=mh, cc=cc, sl=sl: e.matmul(
                        ps_g[sl2][:, 0:256], sgw[:, cc, mh * 128:(mh + 1) * 128],
                        y1s[:, sl, cc, :], start=(cc == 0), stop=(cc == 7)),
                        waits=wg if cc == 0 else (), inc=pe if cc == 7 else None)
                    wg = ()
                vg = pe.n
                slu, wu = rot_u.acquire()
                wu = list(wu) + ldsu
                for cc in range(8):
                    P.op("te", lambda e, sl2=slu, mh=mh, cc=cc, sl=sl: e.matmul(
                        ps_u[sl2][:, 0:256], suw[:, cc, mh * 128:(mh + 1) * 128],
                        y1s[:, sl, cc, :], start=(cc == 0), stop=(cc == 7)),
                        waits=wu if cc == 0 else (), inc=pe if cc == 7 else None)
                    wu = ()
                vu = pe.n
                sgsl = mh % 2
                P.op("se", lambda e, sl2=slg, sgsl=sgsl: e.activation(
                    out=sgs[:, sgsl, :], in_=ps_g[sl2][:, 0:256], func=AF.Silu),
                    waits=[(pe, vg), (ve, h_prev3[sgsl])], inc=se)
                vsg = se.n
                rot_g.release(slg, se, se.n)
                P.op("ve", lambda e, sl2=slu, mh=mh, sgsl=sgsl: e.tensor_tensor(
                    out=h_sb[:, mh, :], in0=sgs[:, sgsl, :],
                    in1=ps_u[sl2][:, 0:256], op=OP.mult),
                    waits=[(se, vsg), (pe, vu), (pe, prev_down_pe)], inc=ve)
                h_ve[mh] = ve.n
                h_prev3[sgsl] = ve.n
                rot_u.release(slu, ve, ve.n)
            y1s_rel[sl] = (pe, pe.n)
            for mt in range(2):
                sld = (tb * 2 + mt) % 2
                wmm = [(ve, h_ve[15])] + ldsd
                if psd_rel[sld] is not None:
                    wmm.append(psd_rel[sld])
                for kc in range(16):
                    for nh in range(2):
                        first = (kc == 0)
                        last = (kc == 15)
                        P.op("te", lambda e, kc=kc, nh=nh, mt=mt, first=first,
                             last=last, sld=sld: e.matmul(
                            ps_d[sld][:, nh * 512:(nh + 1) * 512],
                            h_sb[:, kc, mt * 128:(mt + 1) * 128],
                            sdw[:, kc, nh * 512:(nh + 1) * 512],
                            start=first, stop=last),
                            waits=wmm if first else (), inc=pe if last else None)
                        wmm = ()
                vd = pe.n
                prev_down_pe = vd
                sl2 = mt % 2
                sl4 = (tb * 2 + mt) % 4
                wy = list(self.tk_rysp_all)
                if ryr_rel[sl4] is not None:
                    wy.append(ryr_rel[sl4])
                ldr0 = st_ryr.dma(P, "sy", ryr[:, sl4, 0:512],
                                  d["routedy_sp"][tb * 2 + mt][:, 0:512],
                                  waits=wy)
                ldr1 = st_ryr.dma(P, "sy", ryr[:, sl4, 512:1024],
                                  d["routedy_sp"][tb * 2 + mt][:, 512:1024],
                                  waits=wy)
                wv2 = [(pe, vd), ldr0, ldr1]
                if y2p_rel[sl2] is not None:
                    wv2 += y2p_rel[sl2]
                P.op("ve", lambda e, sl2=sl2, sl4=sl4, sld=sld: e.tensor_tensor(
                    out=y2p[:, sl2, :], in0=ps_d[sld][:], in1=ryr[:, sl4, :],
                    op=OP.add), waits=wv2, inc=ve)
                ryr_rel[sl4] = (ve, ve.n)
                psd_rel[sld] = (ve, ve.n)
                vy2 = ve.n
                tk = st_y2sp.dma(P, "sy", d["y2pre_sp"][tb * 2 + mt],
                                 y2p[:, sl2, :], waits=[(ve, vy2)])
                P.drain("ve")
                P.op("ve", lambda e, sl2=sl2: e.bn_stats(
                    out=bnst[:, sl2, 0:6], in_=y2p[:, sl2, 0:512]))
                P.op("ve", lambda e, sl2=sl2: e.bn_stats(
                    out=bnst[:, sl2, 6:12], in_=y2p[:, sl2, 512:1024]))
                P.drain("ve")
                P.op("ve", lambda e, sl2=sl2: e.bn_aggr(
                    out=mv[:, sl2, :], in_=bnst[:, sl2, :]), inc=ve)
                vmv = ve.n
                P.op("se", lambda e, sl2=sl2: e.activation(
                    out=lnv[:, sl2:sl2 + 1], in_=mv[:, sl2, 1:2],
                    func=AF.Ln, bias=self.epsv[:, 0:1]),
                    waits=[(ve, vmv), self.tk_eps])
                P.drain("se")
                P.op("se", lambda e, sl2=sl2: e.activation(
                    out=rstd[:, sl2:sl2 + 1], in_=lnv[:, sl2:sl2 + 1],
                    func=AF.Exp, scale=-0.5), inc=se)
                vrs = se.n
                P.op("ve", lambda e, sl2=sl2: e.tensor_scalar(
                    out=nmr[:, sl2:sl2 + 1], in0=mv[:, sl2, 0:1],
                    scalar1=rstd[:, sl2:sl2 + 1], scalar2=-1.0,
                    op0=OP.mult, op1=OP.mult), waits=[(se, vrs)], inc=ve)
                vnm = ve.n
                wo = [(ve, vnm)]
                if ob_rel[sl2] is not None:
                    wo.append(ob_rel[sl2])
                P.op("se", lambda e, sl2=sl2: e.activation(
                    out=ob[:, sl2, :], in_=y2p[:, sl2, :], func=AF.Identity,
                    scale=rstd[:, sl2:sl2 + 1], bias=nmr[:, sl2:sl2 + 1]),
                    waits=wo, inc=se)
                vob = se.n
                y2p_rel[sl2] = [tk, (se, vob)]
                tko = st_out.dma(P, "sy",
                                 d["out"][(tb * 2 + mt) * 128:
                                          (tb * 2 + mt + 1) * 128, :],
                                 ob[:, sl2, :], waits=[(se, vob)])
                ob_rel[sl2] = tko

        self.tk_y2sp_all = [(s, s.n) for s in st_y2sp.sems]
        P.wait("sy", *(self.tk_y2sp_all + [(s, s.n) for s in st_out.sems]))
        P.emit(nc, "p3")
        c4.close()

    def phase35(self):
        nc, d, sm = self.nc, self.d, self.sem
        ve, se = (sm[k] for k in ("ve", "se"))
        c5 = ExitStack()
        S = lambda n, s, t: c5.enter_context(nc.sbuf_tensor(n, s, t))
        y2r = S("y2r", [128, 2, 1024], F32)
        bnst = S("bnst5", [128, 12], F32)
        mv = S("mv5", [128, 2], F32)
        lnv = S("lnv5", [128, 1], F32)
        rstd = S("rstd5", [128, 1], F32)
        nmr = S("nmr5", [128, 1], F32)
        ob = S("ob", [128, 2, 1024], F32)
        P = Plan()
        st_y2r = self.new_stream("y2r", 2)
        st_out = self.new_stream("outs", 2)
        y2r_rel = [None, None]
        ob_rel = [None, None]
        for t in range(16):
            sl = t % 2
            w = list(self.tk_y2sp_all)
            if y2r_rel[sl] is not None:
                w.append(y2r_rel[sl])
            ldt = st_y2r.dma(P, "sy", y2r[:, sl, :], d["y2pre_sp"][t], waits=w)
            P.op("ve", lambda e, sl=sl: e.bn_stats(out=bnst[:, 0:6],
                                                   in_=y2r[:, sl, 0:512]),
                 waits=[ldt])
            P.op("ve", lambda e, sl=sl: e.bn_stats(out=bnst[:, 6:12],
                                                   in_=y2r[:, sl, 512:1024]))
            P.drain("ve")
            P.op("ve", lambda e: e.bn_aggr(out=mv[:], in_=bnst[:]), inc=ve)
            vmv = ve.n
            P.op("se", lambda e: e.activation(out=lnv[:], in_=mv[:, 1:2],
                                              func=AF.Ln, bias=self.epsv[:, 0:1]),
                 waits=[(ve, vmv), self.tk_eps])
            P.drain("se")
            P.op("se", lambda e: e.activation(out=rstd[:], in_=lnv[:],
                                              func=AF.Exp, scale=-0.5), inc=se)
            vrs = se.n
            P.op("ve", lambda e: e.tensor_scalar(
                out=nmr[:], in0=mv[:, 0:1], scalar1=rstd[:, 0:1], scalar2=-1.0,
                op0=OP.mult, op1=OP.mult), waits=[(se, vrs)], inc=ve)
            vnm = ve.n
            wo = [(ve, vnm)]
            if ob_rel[sl] is not None:
                wo.append(ob_rel[sl])
            P.op("se", lambda e, sl=sl: e.activation(
                out=ob[:, sl, :], in_=y2r[:, sl, :], func=AF.Identity,
                scale=rstd[:, 0:1], bias=nmr[:, 0:1]), waits=wo, inc=se)
            y2r_rel[sl] = (se, se.n)
            vob = se.n
            tk = st_out.dma(P, "sy", d["out"][t * 128:(t + 1) * 128, :],
                            ob[:, sl, :], waits=[(se, vob)])
            ob_rel[sl] = tk
        P.wait("sy", *[(s, s.n) for s in st_out.sems])
        P.emit(nc, "p35")
        c5.close()

    def build(self, upto=99):
        self.declare()
        with self.ctx:
            self.phase0()
            if upto >= 1:
                self.phase1()
            if upto >= 2:
                self.phase15()
            if upto >= 3:
                self.phase2()
            if upto >= 4:
                self.phase3()
            for stk in (getattr(self, "ce", None), getattr(self, "ca", None)):
                if stk is not None:
                    stk.close()
            self.ce = self.ca = None
        return self.nc


def _host_consts():
    NEG = np.float32(NEGF)
    jj = np.arange(128)
    qq = np.arange(128)[:, None]
    maskA = np.where(jj[None, :] >= qq, 0.0, NEG).astype(np.float32)
    maskB = np.where(jj[None, :] < qq, 0.0, NEG).astype(np.float32)
    maskAB = np.concatenate([maskA, maskB], axis=1)
    ident = np.eye(128, dtype=np.float32)
    zpool = np.zeros((128, 129), np.float32)
    zpool[:, 64] = 1.0 / 256.0
    rep2 = np.zeros((2, 128), np.float32)
    rep2[0, 0:64] = 0.5
    rep2[1, 64:128] = 0.5
    ue16 = np.zeros((16, 2048), np.float32)
    for e in range(16):
        ue16[e, e * 128:(e + 1) * 128] = 1.0
    return maskAB, maskA, maskB, ident, zpool, rep2, ue16


def make_in_maps(inputs):
    f32c = lambda a: np.ascontiguousarray(np.asarray(a, np.float32))
    x = f32c(inputs["L0_attn_x"]).reshape(16384, 1024)
    qkv_b = f32c(inputs["L0_attn_qkv_b"])
    maskAB, maskA, maskB, ident, zpool, rep2, ue16 = _host_consts()
    glgr = np.zeros((128, 4), np.float32)
    gl = f32c(inputs["L0_attn_gl_w"])[:, 0]
    gr = f32c(inputs["L0_attn_gr_w"])[:, 0]
    glgr[0:64, 0] = gl
    glgr[64:128, 1] = gl
    glgr[0:64, 2] = gr
    glgr[64:128, 3] = gr
    base = {
        "x_all": x,
        "qkv_w": f32c(inputs["L0_attn_qkv_w"]),
        "rq_w": f32c(inputs["L0_attn_rq_w"]),
        "rkv_w": f32c(inputs["L0_attn_rkv_w"]),
        "rout_w": f32c(inputs["L0_attn_rout_w"]),
        "out_w": f32c(inputs["L0_attn_out_w"]),
        "router_wT": np.ascontiguousarray(f32c(inputs["L0_ffn_router_w"]).T),
        "gate_w": f32c(inputs["L0_ffn_gate_w"]),
        "up_w": f32c(inputs["L0_ffn_up_w"]),
        "down_w": f32c(inputs["L0_ffn_down_w"]),
        "sgw_t": np.ascontiguousarray(f32c(inputs["L0_ffn_shared_gate_w"]).T),
        "suw_t": np.ascontiguousarray(f32c(inputs["L0_ffn_shared_up_w"]).T),
        "sdw_t": np.ascontiguousarray(f32c(inputs["L0_ffn_shared_down_w"]).T),
        "ident_in": ident,
        "zpool_in": zpool,
        "rep2_in": rep2,
        "ue16_in": ue16,
        "glgr_in": glgr,
        "maskAB_in": maskAB,
        "qkvbt_in": np.ascontiguousarray(qkv_b.reshape(24, 128).T),
        "rqbt_in": np.ascontiguousarray(
            f32c(inputs["L0_attn_rq_b"]).reshape(2, 128).T),
        "rkvbk_in": np.ascontiguousarray(
            f32c(inputs["L0_attn_rkv_b"])[:256].reshape(2, 128).T),
        "rvb_in": np.ascontiguousarray(
            np.tile(f32c(inputs["L0_attn_rkv_b"])[256:][None, :], (64, 1))),
        "routbt_in": np.ascontiguousarray(
            f32c(inputs["L0_attn_rout_b"]).reshape(8, 128).T),
        "outbrep_in": np.ascontiguousarray(
            np.tile(f32c(inputs["L0_attn_out_b"])[None, :], (128, 1))),
        "vbrep_in": np.ascontiguousarray(np.tile(qkv_b[2048:][None, :], (128, 1))),
        "corr_in": np.ascontiguousarray(
            np.tile(f32c(inputs["L0_ffn_correction_bias"])[None, :], (128, 1))),
    }
    NEG = np.float32(NEGF)
    in_maps = []
    for c in range(8):
        t0 = c * 2048
        m = dict(base)
        m["x_own"] = np.ascontiguousarray(x[t0:t0 + 2048])
        m["x_halo"] = (np.zeros((128, 1024), np.float32) if c == 0
                       else np.ascontiguousarray(x[t0 - 128:t0]))
        mA0 = np.full((128, 128), NEG, np.float32) if c == 0 else maskA
        m["maskA0B_in"] = np.ascontiguousarray(np.concatenate([mA0, maskB], 1))
        lat = np.zeros((16, 64), np.float32)
        for qt in range(16):
            cmax = (t0 + qt * 128) // 256
            lat[qt] = np.where(np.arange(64) < cmax, 0.0, NEG)
        m["latmask_in"] = np.ascontiguousarray(
            np.broadcast_to(lat[None, :, :], (128, 16, 64)).copy())
        in_maps.append(m)
    return in_maps


def kernel(**inputs):
    in_maps = make_in_maps(inputs)
    nc = Builder().build()
    res = run_bass_kernel_spmd(nc, in_maps, core_ids=list(range(8)))
    out = np.concatenate([r["out"] for r in res.results], axis=0)
    return out.reshape(1, 16384, 1024).astype(np.float32)
